# version 01
# baseline (speedup 1.0000x reference)
import contextlib
import numpy as np
import ml_dtypes
import concourse.bass as bass
import concourse.mybir as mybir
from concourse.bass_utils import run_bass_kernel_spmd

F32 = mybir.dt.float32
BF16 = mybir.dt.bfloat16
AF = mybir.ActivationFunctionType
ALU = mybir.AluOpType
AXX = mybir.AxisListType.X
BF = ml_dtypes.bfloat16

C = 1024
SL = 8
H = 16
HK = 64
CH = 128
MAC = 512
CPM = MAC // CH
T_OWN = 2048
N_CORE = 8
EPS_GN = 1e-5 * 64.0
EPS_LN = 1e-5
NE = 9
F_FF = 2048

_DEBUG = []
_MAX_ON = None
_Y_TERMS = 3
_ON_TRACE = []
_N_PRE = 12
_N_OWN = 4
_DO_FFN = True
_LAST_RES = [None]


def _mk(x):
    return np.ascontiguousarray(x)


def build_program(n_mac_pre=12, n_mac_own=4, do_ffn=True, debug=()):
    nc = bass.Bass("TRN2", target_bir_lowering=False, detect_race_conditions=False)
    NMAC = n_mac_pre + n_mac_own
    NCH = NMAC * CPM
    T_WIN = NMAC * MAC

    d_x = nc.dram_tensor("xin", [T_WIN + 1, C], F32, kind="ExternalInput").ap()
    d_wk = nc.dram_tensor("W_k", [SL, 128, C], F32, kind="ExternalInput").ap()
    d_wv = nc.dram_tensor("W_v", [SL, 128, C], F32, kind="ExternalInput").ap()
    d_wr = nc.dram_tensor("W_r", [SL, 128, C], F32, kind="ExternalInput").ap()
    d_wg = nc.dram_tensor("W_g", [SL, 128, C], F32, kind="ExternalInput").ap()
    d_wo = nc.dram_tensor("W_o", [SL, 128, C], F32, kind="ExternalInput").ap()
    d_w1 = nc.dram_tensor("maa_w1", [SL, 128, 160], F32, kind="ExternalInput").ap()
    d_w2 = nc.dram_tensor("w2aug", [33, 5, C], F32, kind="ExternalInput").ap()
    d_d1 = nc.dram_tensor("dec_w1", [SL, 128, 64], F32, kind="ExternalInput").ap()
    d_d2 = nc.dram_tensor("dec_w2", [64, C], F32, kind="ExternalInput").ap()
    d_tv = nc.dram_tensor("tvec", [128, SL, 3], F32, kind="ExternalInput").ap()
    d_gb = nc.dram_tensor("lngb", [128, 2, C], F32, kind="ExternalInput").ap()
    d_rw = nc.dram_tensor("router_wT", [SL, 128, 8], F32, kind="ExternalInput").ap()
    d_rb = nc.dram_tensor("router_b", [8, 1], F32, kind="ExternalInput").ap()
    d_up = nc.dram_tensor("wupT", [NE, SL, 128, F_FF], F32, kind="ExternalInput").ap()
    d_dn = nc.dram_tensor("wdnT", [NE, 16, 128, C], F32, kind="ExternalInput").ap()
    d_out = nc.dram_tensor("out", [T_OWN, C], F32, kind="ExternalOutput").ap()

    dbg_out = {}
    for name, shape in debug:
        dbg_out[name] = nc.dram_tensor("dbg_" + name, list(shape), F32, kind="ExternalOutput").ap()

    i_idb = nc.inline_tensor(np.eye(128, dtype=BF), "idb")
    i_idf = nc.inline_tensor(np.eye(128, dtype=np.float32), "idf")
    i_mask = nc.inline_tensor(np.triu(np.ones((128, 128), dtype=BF), 1), "masksu")
    i_ones = nc.inline_tensor(np.ones((128, 64), dtype=BF), "ones64")

    with contextlib.ExitStack() as ctx:
        E = ctx.enter_context

        sems, cnt = {}, {}
        for s in ["c0", "x0", "x1", "xe0", "xe1", "wd", "ws", "t", "v", "s", "g", "st", "xf", "w0", "w1", "db"]:
            sems[s] = E(nc.semaphore("sem_" + s))
            cnt[s] = 0

        def w(eng, name, val=None):
            v = cnt[name] if val is None else val
            assert v <= cnt[name], ("forward wait", name, v, cnt[name])
            eng.wait_ge(sems[name], v)

        s_x1 = E(nc.sbuf_tensor("s_x1", [128, SL, T_OWN], BF16))
        s_id = E(nc.sbuf_tensor("s_id", [128, 128], BF16))
        s_mkk = E(nc.sbuf_tensor("s_mkk", [128, 128], BF16))
        s_o64 = E(nc.sbuf_tensor("s_o64", [128, 64], BF16))
        s_ep1 = E(nc.sbuf_tensor("s_ep1", [128, C], F32))
        s_ep2 = E(nc.sbuf_tensor("s_ep2", [128, C], BF16))
        s_zg = E(nc.sbuf_tensor("s_zg", [128, C], BF16))
        s_st3 = E(nc.sbuf_tensor("s_st3", [128, 4], F32))

        p_A = E(nc.psum_tensor("p_A", [128, 512], F32))
        p_B = E(nc.psum_tensor("p_B", [128, 512], F32))
        p_Y = E(nc.psum_tensor("p_Y", [128, 1024], F32))
        p_Z = E(nc.psum_tensor("p_Z", [128, 1024], F32))
        p_T = E(nc.psum_tensor("p_T", [128, 512], BF16))

        block = E(nc.Block())

        _onctr = [0]

        def on(engname, f):
            import sys as _sys
            _onctr[0] += 1
            _ON_TRACE.append((_onctr[0], engname, _sys._getframe(1).f_lineno))
            if _MAX_ON is not None and _onctr[0] > _MAX_ON:
                return
            getattr(block, engname)(f)

        inner = ctx.enter_context(contextlib.ExitStack())
        E2 = inner.enter_context

        s_wk = E2(nc.sbuf_tensor("s_wk", [128, SL, C], BF16))
        s_wv = E2(nc.sbuf_tensor("s_wv", [128, SL, C], BF16))
        s_ws = E2(nc.sbuf_tensor("s_ws", [128, SL, C], BF16))
        s_w1 = E2(nc.sbuf_tensor("s_w1", [128, SL, 160], BF16))
        s_w2 = E2(nc.sbuf_tensor("s_w2", [33, 5, C], BF16))
        s_d1 = E2(nc.sbuf_tensor("s_d1", [128, SL, 64], BF16))
        s_d2 = E2(nc.sbuf_tensor("s_d2", [64, C], BF16))
        s_tv = E2(nc.sbuf_tensor("s_tv", [128, SL, 3], F32))
        s_gb = E2(nc.sbuf_tensor("s_gb", [128, 2, C], BF16))
        s_n1 = E2(nc.sbuf_tensor("s_n1", [128, CH], F32))
        s_S = E2(nc.sbuf_tensor("s_S", [128, SL, HK], F32))
        s_Sc = E2(nc.sbuf_tensor("s_Sc", [128, SL, HK], BF16))

        s_xf = E2(nc.sbuf_tensor("s_xf", [128, SL, MAC], BF16))
        s_xx = E2(nc.sbuf_tensor("s_xx", [128, SL, MAC], BF16))
        s_mx = E2(nc.sbuf_tensor("s_mx", [128, SL, MAC], BF16))
        s_rf = E2(nc.sbuf_tensor("s_rf", [128, SL, MAC], BF16))
        s_kf = E2(nc.sbuf_tensor("s_kf", [128, SL, MAC], BF16))
        s_vt = E2(nc.sbuf_tensor("s_vt", [128, CPM, C], BF16))
        s_gg = E2(nc.sbuf_tensor("s_gg", [128, CPM, C], BF16))
        s_m5 = E2(nc.sbuf_tensor("s_m5", [33, 5, MAC], BF16))
        s_m6 = E2(nc.sbuf_tensor("s_m6", [64, MAC], BF16))
        s_pc = E2(nc.sbuf_tensor("s_pc", [128, SL, 1], BF16))

        s_xtm = E2(nc.sbuf_tensor("s_xtm", [128, 2, C], BF16))
        s_xep = E2(nc.sbuf_tensor("s_xep", [128, 2, C], BF16))
        s_ew = E2(nc.sbuf_tensor("s_ew", [128, SL, CH], F32))
        s_L = E2(nc.sbuf_tensor("s_L", [128, SL, CH + 1], F32))
        s_D1 = E2(nc.sbuf_tensor("s_D1", [128, SL, CH], BF16))
        s_D2 = E2(nc.sbuf_tensor("s_D2", [128, SL, CH], BF16))
        s_Rp = E2(nc.sbuf_tensor("s_Rp", [128, SL, CH], BF16))
        s_Kp = E2(nc.sbuf_tensor("s_Kp", [128, SL, CH], BF16))
        s_Kt = E2(nc.sbuf_tensor("s_Kt", [128, SL, CH], BF16))
        s_P2 = E2(nc.sbuf_tensor("s_P2", [128, SL, CH], BF16))
        s_M = E2(nc.sbuf_tensor("s_M", [128, 4, CH], BF16))
        s_ti = E2(nc.sbuf_tensor("s_ti", [128, SL, 2], F32))
        s_te = E2(nc.sbuf_tensor("s_te", [128, SL, 2], F32))
        s_zf = E2(nc.sbuf_tensor("s_zf", [128, SL, 128], BF16))
        s_st1 = E2(nc.sbuf_tensor("s_st1", [128, H], F32))
        s_st2 = E2(nc.sbuf_tensor("s_st2", [128, H], F32))
        s_dg = E2(nc.sbuf_tensor("s_dg", [128, H], F32))

        def f(sync):
            li = sync.dma_start(out=s_tv[:], in_=d_tv).then_inc(sems["c0"], 16)
            li = sync.dma_start(out=s_id[:], in_=i_idb.ap()).then_inc(sems["c0"], 16)
            li = sync.dma_start(out=s_mkk[:], in_=i_mask.ap()).then_inc(sems["c0"], 16)
            li = sync.dma_start(out=s_o64[:], in_=i_ones.ap()).then_inc(sems["c0"], 16)
        on("sync", f)
        cnt["c0"] += 4 * 16

        def f(gpsimd):
            for dst, src in [(s_wk, d_wk), (s_wv, d_wv), (s_w1, d_w1), (s_d1, d_d1)]:
                li = gpsimd.dma_start(out=dst[:], in_=src.rearrange("s p c -> p s c")).then_inc(sems["wd"], 16)
            li = gpsimd.dma_start(out=s_w2[:], in_=d_w2).then_inc(sems["wd"], 16)
            li = gpsimd.dma_start(out=s_d2[:], in_=d_d2).then_inc(sems["wd"], 16)
            li = gpsimd.dma_start(out=s_gb[:], in_=d_gb).then_inc(sems["wd"], 16)
            li = gpsimd.dma_start(out=s_pc[:], in_=d_x[0:1, :].rearrange("r (s p) -> p s r", p=128),
                             allow_slow_non_contiguous=True).then_inc(sems["wd"], 16)
        on("gpsimd", f)
        cnt["wd"] += 8 * 16

        def f(vector):
            w(vector, "c0")
            w(vector, "wd")
            li = vector.memset(s_n1[:], -1.0)
            li = vector.memset(s_S[:], 0.0)
            li = vector.memset(s_L[:, :, 0:1], 0.0)
            li = vector.memset(s_m5[32:33, :, :], 1.0)
            li.then_inc(sems["v"], 1)
        on("vector", f)
        cnt["v"] += 1
        v_init = cnt["v"]

        xw_after = {}

        def emit_xload(g):
            par = g % 2
            sname = "x%d" % par

            def f(gpsimd, g=g, par=par, sname=sname):
                if g >= 2:
                    w(gpsimd, "xf", g - 1)
                li = gpsimd.dma_start(out=s_xtm[:, par, :],
                                 in_=d_x[1 + g * CH: 1 + (g + 1) * CH, :]).then_inc(sems[sname], 16)
            on("gpsimd", f)
            cnt[sname] += 16
            xw_after[g] = cnt[sname]

        def dbg(name, ap_src):
            if name in dbg_out:
                def f(gpsimd):
                    for sn in ["t", "v", "s", "g"]:
                        w(gpsimd, sn)
                    li = gpsimd.dma_start(out=dbg_out[name], in_=ap_src).then_inc(sems["db"], 16)
                on("gpsimd", f)
                cnt["db"] += 16
                for eng in ["tensor", "vector", "scalar"]:
                    def f(e):
                        w(e, "db")
                    on(eng, f)

        emit_xload(0)
        emit_xload(1)

        TGT = ["w", "k", "v", "r", "g"]

        for m in range(NMAC):
            own = m >= n_mac_pre
            tgts = TGT if own else TGT[:3]

            if own:
                def f(gpsimd):
                    w(gpsimd, "t")
                    li = gpsimd.dma_start(out=s_ws[:], in_=d_wr.rearrange("s p c -> p s c")).then_inc(sems["ws"], 16)
                on("gpsimd", f)
                cnt["ws"] += 16
            ws_r = cnt["ws"]

            for cc in range(CPM):
                g = m * CPM + cc
                par = g % 2
                xname = "x%d" % par
                for half in range(2):
                    def f(tensor, g=g, par=par, half=half, xname=xname, m=m, cc=cc):
                        if m == 0 and cc == 0 and half == 0:
                            w(tensor, "v", v_init)
                        w(tensor, xname, xw_after[g])
                        w(tensor, "s")
                        if half == 0 and cc == 0:
                            w(tensor, "v")
                            w(tensor, "g")
                        for i in range(4):
                            s = half * 4 + i
                            li = tensor.transpose(p_T[:, i * 128:(i + 1) * 128],
                                             s_xtm[:, par, 128 * s:128 * (s + 1)], s_id[:])
                        li.then_inc(sems["t"], 1)
                    on("tensor", f)
                    cnt["t"] += 1
                    tval = cnt["t"]

                    def f(scalar, cc=cc, half=half, tval=tval):
                        w(scalar, "t", tval)
                        li = scalar.copy(s_xf[:, half * 4:(half + 1) * 4, cc * CH:(cc + 1) * CH],
                                    p_T[:].rearrange("p (s c) -> p s c", s=4))
                        li.then_inc(sems["s"], 1)
                    on("scalar", f)
                    cnt["s"] += 1
                s_xtr = cnt["s"]

                def f(gpsimd, s_xtr=s_xtr):
                    w(gpsimd, "s", s_xtr)
                    gpsimd.nop().then_inc(sems["xf"], 1)
                on("gpsimd", f)
                cnt["xf"] += 1
                if g + 2 < NCH:
                    emit_xload(g + 2)

            def f(vector):
                w(vector, "s")
                w(vector, "g")
                li = vector.tensor_sub(s_xx[:, :, 1:MAC], s_xf[:, :, 0:MAC - 1], s_xf[:, :, 1:MAC])
                li = vector.tensor_sub(s_xx[:, :, 0:1], s_pc[:], s_xf[:, :, 0:1])
                li.then_inc(sems["v"], 1)
                cnt["v"] += 1
                vector.wait_ge(sems["v"], cnt["v"])
                for s in range(SL):
                    li = vector.scalar_tensor_tensor(s_mx[:, s, :], s_xx[:, s, :], s_tv[:, s, 0:1],
                                                s_xf[:, s, :], ALU.mult, ALU.add)
                li = vector.tensor_copy(s_pc[:], s_xf[:, :, MAC - 1:MAC])
                li.then_inc(sems["v"], 1)
            on("vector", f)
            cnt["v"] += 1
            v_xxx = cnt["v"]

            for t_i, tgt in enumerate(TGT):
                if tgt not in tgts:
                    continue
                def f(tensor, t_i=t_i, v_xxx=v_xxx):
                    w(tensor, "v", v_xxx)
                    w(tensor, "s")
                    for dc in range(SL):
                        li = tensor.matmul(p_A[0:32, 0:MAC], s_w1[:, dc, 32 * t_i:32 * (t_i + 1)],
                                      s_mx[:, dc, :], start=(dc == 0), stop=(dc == SL - 1))
                    li.then_inc(sems["t"], 1)
                on("tensor", f)
                cnt["t"] += 1
                tval = cnt["t"]

                def f(scalar, t_i=t_i, tval=tval):
                    w(scalar, "t", tval)
                    li = scalar.activation(s_m5[0:32, t_i, :], p_A[0:32, 0:MAC], AF.Tanh)
                    li.then_inc(sems["s"], 1)
                on("scalar", f)
                cnt["s"] += 1
            s_m5d = cnt["s"]

            for t_i, tgt in enumerate(TGT):
                if tgt not in tgts:
                    continue
                for s in range(SL):
                    def f(tensor, t_i=t_i, s=s):
                        w(tensor, "s")
                        w(tensor, "g")
                        if s == 0:
                            w(tensor, "v")
                        pp = p_A if s % 2 == 0 else p_Z
                        li = tensor.matmul(pp[:, 0:MAC], s_w2[:, t_i, 128 * s:128 * (s + 1)],
                                      s_m5[:, t_i, :], start=True, stop=True)
                        li.then_inc(sems["t"], 1)
                    on("tensor", f)
                    cnt["t"] += 1
                    tval = cnt["t"]

                    def f(vector, s=s, tval=tval):
                        w(vector, "t", tval)
                        pp = p_A if s % 2 == 0 else p_Z
                        li = vector.tensor_mul(s_mx[:, s, :], pp[:, 0:MAC], s_xx[:, s, :])
                        li.then_inc(sems["v"], 1)
                    on("vector", f)
                    cnt["v"] += 1
                    vval = cnt["v"]

                    def f(gpsimd, s=s, vval=vval):
                        w(gpsimd, "v", vval)
                        li = gpsimd.tensor_add(s_mx[:, s, :], s_mx[:, s, :], s_xf[:, s, :])
                        li.then_inc(sems["g"], 1)
                    on("gpsimd", f)
                    cnt["g"] += 1
                g_mix = cnt["g"]

                if tgt == "w":
                    def f(tensor, g_mix=g_mix):
                        w(tensor, "g", g_mix)
                        w(tensor, "s")
                        for dc in range(SL):
                            li = tensor.matmul(p_A[0:64, 0:MAC], s_d1[:, dc, :], s_mx[:, dc, :],
                                          start=(dc == 0), stop=(dc == SL - 1))
                        li.then_inc(sems["t"], 1)
                    on("tensor", f)
                    cnt["t"] += 1
                    tval = cnt["t"]

                    def f(scalar, tval=tval):
                        w(scalar, "t", tval)
                        li = scalar.activation(s_m6[:], p_A[0:64, 0:MAC], AF.Tanh)
                        li.then_inc(sems["s"], 1)
                    on("scalar", f)
                    cnt["s"] += 1
                elif tgt in ("k", "r"):
                    dst = s_kf if tgt == "k" else s_rf
                    for o in range(SL):
                        def f(tensor, o=o, g_mix=g_mix, tgt=tgt):
                            w(tensor, "g", g_mix)
                            w(tensor, "v")
                            if tgt == "r" and o == 0:
                                w(tensor, "ws", ws_r)
                            wmat = s_wk if tgt == "k" else s_ws
                            pp = p_B if o % 2 == 0 else p_A
                            for dc in range(SL):
                                li = tensor.matmul(pp[:, 0:MAC], wmat[:, dc, 128 * o:128 * (o + 1)],
                                              s_mx[:, dc, :], start=(dc == 0), stop=(dc == SL - 1))
                            li.then_inc(sems["t"], 1)
                        on("tensor", f)
                        cnt["t"] += 1
                        tval = cnt["t"]

                        def f(vector, o=o, tval=tval, dst=dst):
                            w(vector, "t", tval)
                            li = vector.tensor_copy(dst[:, o, :], (p_B if o % 2 == 0 else p_A)[:, 0:MAC])
                            li.then_inc(sems["v"], 1)
                        on("vector", f)
                        cnt["v"] += 1
                else:
                    wm = s_wv if tgt == "v" else s_ws
                    dstt = s_vt if tgt == "v" else s_gg
                    if tgt == "g":
                        def f(gpsimd, g_mix=g_mix):
                            w(gpsimd, "t")
                            li = gpsimd.dma_start(out=s_ws[:], in_=d_wg.rearrange("s p c -> p s c")).then_inc(sems["ws"], 16)
                        on("gpsimd", f)
                        cnt["ws"] += 16
                    ws_tg = cnt["ws"]
                    for cc in range(CPM):
                        def f(tensor, cc=cc, g_mix=g_mix, wm=wm, tgt=tgt, ws_tg=ws_tg):
                            w(tensor, "g", g_mix)
                            if tgt == "g" and cc == 0:
                                w(tensor, "ws", ws_tg)
                            w(tensor, "v")
                            w(tensor, "s")
                            for dc in range(SL):
                                for hh in range(2):
                                    li = tensor.matmul(p_Y[:, hh * 512:(hh + 1) * 512],
                                                  s_mx[:, dc, cc * CH:(cc + 1) * CH],
                                                  wm[:, dc, hh * 512:(hh + 1) * 512],
                                                  start=(dc == 0), stop=(dc == SL - 1))
                            li.then_inc(sems["t"], 1)
                        on("tensor", f)
                        cnt["t"] += 1
                        tval = cnt["t"]
                        if tgt == "v":
                            def f(vector, cc=cc, tval=tval):
                                w(vector, "t", tval)
                                li = vector.tensor_copy(s_vt[:, cc, :], p_Y[:])
                                li.then_inc(sems["v"], 1)
                            on("vector", f)
                            cnt["v"] += 1
                        else:
                            def f(scalar, cc=cc, tval=tval):
                                w(scalar, "t", tval)
                                li = scalar.activation(s_gg[:, cc, :], p_Y[:], AF.Silu)
                                li.then_inc(sems["s"], 1)
                            on("scalar", f)
                            cnt["s"] += 1

            if own:
                def f(gpsimd):
                    w(gpsimd, "t")
                    li = gpsimd.dma_start(out=s_ws[:], in_=d_wo.rearrange("s p c -> p s c")).then_inc(sems["ws"], 16)
                on("gpsimd", f)
                cnt["ws"] += 16
            ws_o = cnt["ws"]

            dbg("xf_m%d" % m, s_xf[:])
            dbg("xx_m%d" % m, s_xx[:])
            dbg("rf_m%d" % m, s_rf[:])
            dbg("kf_m%d" % m, s_kf[:])
            dbg("vt_m%d" % m, s_vt[:])
            dbg("gg_m%d" % m, s_gg[:])
            dbg("m6_m%d" % m, s_m6[:])

            for cc in range(CPM):
                g = m * CPM + cc
                if own:
                    epar = cc % 2
                    ename = "xe%d" % epar

                    def f(gpsimd, g=g, epar=epar, ename=ename):
                        w(gpsimd, "v")
                        li = gpsimd.dma_start(out=s_xep[:, epar, :],
                                         in_=d_x[1 + g * CH:1 + (g + 1) * CH, :]).then_inc(sems[ename], 16)
                    on("gpsimd", f)
                    cnt[ename] += 16
                    xe_w = cnt[ename]

                for q in range(2):
                    def f(tensor, cc=cc, q=q):
                        w(tensor, "s")
                        w(tensor, "v")
                        pp = p_A if q == 0 else p_Z
                        for i in range(4):
                            s = q * 4 + i
                            li = tensor.matmul(pp[:, i * CH:(i + 1) * CH],
                                          s_d2[:, 128 * s:128 * (s + 1)],
                                          s_m6[:, cc * CH:(cc + 1) * CH],
                                          start=True, stop=True)
                        li.then_inc(sems["t"], 1)
                    on("tensor", f)
                    cnt["t"] += 1
                    tval = cnt["t"]

                    def f(vector, q=q, tval=tval):
                        w(vector, "t", tval)
                        pp = p_A if q == 0 else p_Z
                        li = vector.tensor_add(s_ew[:, q * 4:(q + 1) * 4, :],
                                          pp[:, 0:512].rearrange("p (s c) -> p s c", s=4),
                                          s_tv[:, q * 4:(q + 1) * 4, 1:2].broadcast_to([128, 4, CH]))
                        li.then_inc(sems["v"], 1)
                    on("vector", f)
                    cnt["v"] += 1
                    vval = cnt["v"]

                    def f(scalar, q=q, vval=vval):
                        w(scalar, "v", vval)
                        li = scalar.activation(s_ew[:, q * 4:(q + 1) * 4, :],
                                          s_ew[:, q * 4:(q + 1) * 4, :], AF.Exp)
                        li.then_inc(sems["s"], 1)
                    on("scalar", f)
                    cnt["s"] += 1
                s_ewd = cnt["s"]

                def f(vector, s_ewd=s_ewd, own=own):
                    w(vector, "s", s_ewd)
                    for s in range(SL):
                        li = vector.tensor_tensor_scan(s_L[:, s, 1:CH + 1], s_ew[:, s, :],
                                                  s_n1[:], 0.0, ALU.subtract, ALU.mult)
                    li.then_inc(sems["v"], 1)
                    cnt["v"] += 1
                    vector.wait_ge(sems["v"], cnt["v"])
                    li = vector.tensor_scalar_mul(s_ti[:, :, 0:1], s_L[:, :, CH:CH + 1], 0.5)
                    li = vector.tensor_copy(s_ti[:, :, 1:2], s_L[:, :, CH:CH + 1])
                    li.then_inc(sems["v"], 1)
                    cnt["v"] += 1
                    vector.wait_ge(sems["v"], cnt["v"])
                    if own:
                        li = vector.tensor_sub(s_L[:, :, 1:CH + 1], s_L[:, :, 1:CH + 1],
                                          s_ti[:, :, 0:1].broadcast_to([128, SL, CH]))
                        li = vector.tensor_scalar_mul(s_L[:, :, 0:1], s_ti[:, :, 0:1], -1.0)
                    else:
                        li = vector.tensor_sub(s_L[:, :, 1:CH + 1],
                                          s_ti[:, :, 1:2].broadcast_to([128, SL, CH]),
                                          s_L[:, :, 1:CH + 1])
                    li.then_inc(sems["v"], 1)
                on("vector", f)
                cnt["v"] += 1
                v_scan = cnt["v"]
                dbg("ew_c%d" % g, s_ew[:])

                def f(scalar, v_scan=v_scan):
                    w(scalar, "v", v_scan)
                    li = scalar.activation(s_te[:], s_ti[:], AF.Exp)
                    li.then_inc(sems["s"], 1)
                on("scalar", f)
                cnt["s"] += 1

                def f(scalar, own=own, v_scan=v_scan):
                    w(scalar, "v", v_scan)
                    if own:
                        li = scalar.activation(s_D1[:], s_L[:, :, 0:CH], AF.Exp)
                        li = scalar.activation(s_D2[:], s_L[:, :, 1:CH + 1], AF.Exp, scale=-1.0)
                    else:
                        li = scalar.activation(s_D2[:], s_L[:, :, 1:CH + 1], AF.Exp)
                    li.then_inc(sems["s"], 1)
                on("scalar", f)
                cnt["s"] += 1
                s_dd = cnt["s"]
                dbg("D2_c%d" % g, s_D2[:])
                dbg("te_c%d" % g, s_te[:])
                dbg("ti_c%d" % g, s_ti[:])
                dbg("La_c%d" % g, s_L[:])

                def f(vector, cc=cc, own=own, s_dd=s_dd):
                    w(vector, "s", s_dd)
                    w(vector, "t")
                    sl_tok = slice(cc * CH, (cc + 1) * CH)
                    li = vector.tensor_mul(s_Kp[:], s_kf[:, :, sl_tok], s_D2[:])
                    if own:
                        li = vector.tensor_mul(s_Rp[:], s_rf[:, :, sl_tok], s_D1[:])
                        for s in range(SL):
                            li = vector.scalar_tensor_tensor(s_P2[:, s, :], s_rf[:, s, sl_tok],
                                                        s_tv[:, s, 2:3], s_kf[:, s, sl_tok],
                                                        ALU.mult, ALU.mult)
                        li = vector.tensor_mul(s_Sc[:], s_S[:],
                                          s_te[:, :, 0:1].broadcast_to([128, SL, HK]))
                    li.then_inc(sems["v"], 1)
                on("vector", f)
                cnt["v"] += 1
                v_rk = cnt["v"]

                for half in range(2):
                    def f(tensor, half=half, v_rk=v_rk):
                        w(tensor, "v", v_rk)
                        w(tensor, "s")
                        for i in range(4):
                            s = half * 4 + i
                            li = tensor.transpose(p_T[:, i * 128:(i + 1) * 128], s_Kp[:, s, :], s_id[:])
                        li.then_inc(sems["t"], 1)
                    on("tensor", f)
                    cnt["t"] += 1
                    tval = cnt["t"]

                    def f(scalar, half=half, tval=tval):
                        w(scalar, "t", tval)
                        li = scalar.copy(s_Kt[:, half * 4:(half + 1) * 4, :],
                                    p_T[:].rearrange("p (s c) -> p s c", s=4))
                        li.then_inc(sems["s"], 1)
                    on("scalar", f)
                    cnt["s"] += 1
                s_ktd = cnt["s"]

                t_yall = None
                if own:
                    for grp in range(4):
                        heads = [(grp % 2) + 8 * (grp // 2) + 2 * i for i in range(4)]
                        p_M = p_A if grp % 2 == 0 else p_B

                        def f(tensor, grp=grp, heads=heads, p_M=p_M):
                            w(tensor, "v")
                            for i, h in enumerate(heads):
                                s, pb = h // 2, (h % 2) * 64
                                li = tensor.matmul(p_M[:, i * CH:(i + 1) * CH],
                                              s_Kp[pb:pb + 64, s, :], s_Rp[pb:pb + 64, s, :],
                                              start=True, stop=True)
                            li.then_inc(sems["t"], 1)
                        on("tensor", f)
                        cnt["t"] += 1
                        tval = cnt["t"]

                        def f(vector, tval=tval, p_M=p_M):
                            w(vector, "t", tval)
                            li = vector.tensor_mul(s_M[:], p_M[:].rearrange("p (q c) -> p q c", q=4),
                                              s_mkk[:].rearrange("p (q c) -> p q c", q=1).broadcast_to([128, 4, CH]))
                            li.then_inc(sems["v"], 1)
                        on("vector", f)
                        cnt["v"] += 1
                        vval = cnt["v"]

                        def f(tensor, grp=grp, cc=cc, vval=vval, s_ktd=s_ktd, heads=heads):
                            w(tensor, "v", vval)
                            w(tensor, "s", s_ktd)
                            for i, h in enumerate(heads):
                                s, pb = h // 2, (h % 2) * 64
                                co = h * 64
                                li = tensor.matmul(p_Y[:, co:co + 64], s_M[:, i, :],
                                              s_vt[:, cc, co:co + 64], start=True, stop=False)
                                li = tensor.matmul(p_Y[:, co:co + 64], s_Rp[pb:pb + 64, s, :],
                                              s_Sc[pb:pb + 64, s, :], start=False, stop=True)
                            li.then_inc(sems["t"], 1)
                        on("tensor", f)
                        cnt["t"] += 1
                    t_yall = cnt["t"]

                def f(tensor, cc=cc, s_ktd=s_ktd):
                    w(tensor, "s", s_ktd)
                    w(tensor, "v")
                    for s in range(SL):
                        for hh in range(2):
                            li = tensor.matmul(p_B[hh * 64:hh * 64 + 64, s * 64:(s + 1) * 64],
                                          s_Kt[:, s, hh * 64:(hh + 1) * 64],
                                          s_vt[:, cc, (2 * s + hh) * 64:(2 * s + hh + 1) * 64],
                                          start=True, stop=True)
                    li.then_inc(sems["t"], 1)
                on("tensor", f)
                cnt["t"] += 1
                t_state = cnt["t"]

                def f(vector, own=own, t_state=t_state):
                    w(vector, "t", t_state)
                    e1a = s_ep1[:, 0:512].rearrange("p (s j) -> p s j", s=SL)
                    e1b = s_ep1[:, 512:1024].rearrange("p (s j) -> p s j", s=SL)
                    li = vector.tensor_mul(e1a, s_S[:], s_te[:, :, 1:2].broadcast_to([128, SL, HK]))
                    if own:
                        li = vector.tensor_mul(e1b, p_B[:].rearrange("p (s j) -> p s j", s=SL),
                                          s_te[:, :, 0:1].broadcast_to([128, SL, HK]))
                        li.then_inc(sems["v"], 1)
                        cnt["v"] += 1
                        vector.wait_ge(sems["v"], cnt["v"])
                        li = vector.tensor_add(s_S[:], e1a, e1b)
                    else:
                        li = vector.tensor_add(s_S[:], e1a, p_B[:].rearrange("p (s j) -> p s j", s=SL))
                    li.then_inc(sems["v"], 1)
                on("vector", f)
                cnt["v"] += 1

                dbg("L_c%d" % g, s_L[:])
                dbg("S_c%d" % g, s_S[:])

                if not own:
                    continue

                for half in range(2):
                    def f(tensor, half=half, t_yall=t_yall):
                        w(tensor, "s")
                        w(tensor, "v")
                        for i in range(4):
                            s = half * 4 + i
                            li = tensor.transpose(p_T[:, i * 128:(i + 1) * 128], s_P2[:, s, :], s_id[:])
                        li.then_inc(sems["t"], 1)
                    on("tensor", f)
                    cnt["t"] += 1
                    tval = cnt["t"]

                    def f(scalar, half=half, tval=tval):
                        w(scalar, "t", tval)
                        li = scalar.copy(s_Kt[:, half * 4:(half + 1) * 4, :],
                                    p_T[:].rearrange("p (s c) -> p s c", s=4))
                        li.then_inc(sems["s"], 1)
                    on("scalar", f)
                    cnt["s"] += 1
                s_p2t = cnt["s"]

                def f(vector, cc=cc, t_yall=t_yall, s_p2t=s_p2t):
                    w(vector, "t", t_yall)
                    w(vector, "s", s_p2t)
                    li = vector.tensor_reduce(s_dg[:].rearrange("p (h o) -> p h o", o=1),
                                         s_Kt[:].rearrange("p s c -> p (s c)").rearrange("p (h j) -> p h j", h=H),
                                         AXX, ALU.add)
                    li.then_inc(sems["v"], 1)
                    cnt["v"] += 1
                    vector.wait_ge(sems["v"], cnt["v"])
                    li = vector.tensor_mul(s_zg[:].rearrange("p (h j) -> p h j", h=H),
                                      s_vt[:, cc, :].rearrange("p (h j) -> p h j", h=H),
                                      s_dg[:].rearrange("p (h o) -> p h o", o=1).broadcast_to([128, H, HK]))
                    li.then_inc(sems["v"], 1)
                    cnt["v"] += 1
                    vector.wait_ge(sems["v"], cnt["v"])
                    li = vector.tensor_add(s_ep2[:], p_Y[:], s_zg[:])
                    li.then_inc(sems["v"], 1)
                on("vector", f)
                cnt["v"] += 1
                if ("y_c%d" % g) in dbg_out:
                    dbg("y_c%d" % g, s_ep2[:])

                def f(vector):
                    yv = s_ep2[:].rearrange("p (h j) -> p h j", h=H)
                    li = vector.tensor_reduce(s_st1[:].rearrange("p (h o) -> p h o", o=1), yv, AXX, ALU.add)
                    li.then_inc(sems["v"], 1)
                    cnt["v"] += 1
                    vector.wait_ge(sems["v"], cnt["v"])
                    li = vector.tensor_scalar_mul(s_st1[:], s_st1[:], 1.0 / HK)
                    li.then_inc(sems["v"], 1)
                    cnt["v"] += 1
                    vector.wait_ge(sems["v"], cnt["v"])
                    li = vector.tensor_sub(s_ep1[:].rearrange("p (h j) -> p h j", h=H), yv,
                                      s_st1[:].rearrange("p (h o) -> p h o", o=1).broadcast_to([128, H, HK]))
                    li.then_inc(sems["v"], 1)
                on("vector", f)
                cnt["v"] += 1
                vval = cnt["v"]
                dbg("yc_c%d" % g, s_ep1[:])

                def f(scalar, vval=vval):
                    w(scalar, "v", vval)
                    li = scalar.activation(s_zg[:], s_ep1[:], AF.Square)
                    li.then_inc(sems["s"], 1)
                on("scalar", f)
                cnt["s"] += 1
                sval = cnt["s"]
                dbg("sq_c%d" % g, s_zg[:])

                def f(vector, sval=sval):
                    w(vector, "s", sval)
                    li = vector.tensor_reduce(s_st2[:].rearrange("p (h o) -> p h o", o=1),
                                         s_zg[:].rearrange("p (h j) -> p h j", h=H), AXX, ALU.add)
                    li.then_inc(sems["v"], 1)
                    cnt["v"] += 1
                    vector.wait_ge(sems["v"], cnt["v"])
                    li = vector.tensor_scalar(s_st2[:], s_st2[:], 1.0 / HK, EPS_GN, ALU.mult, ALU.add)
                    li.then_inc(sems["v"], 1)
                on("vector", f)
                cnt["v"] += 1
                vval = cnt["v"]
                dbg("st1_c%d" % g, s_st1[:])
                dbg("var_c%d" % g, s_st2[:])

                def f(scalar, vval=vval):
                    w(scalar, "v", vval)
                    li = scalar.activation(s_st2[:], s_st2[:], AF.Ln)
                    li.then_inc(sems["s"], 1)
                    cnt["s"] += 1
                    scalar.wait_ge(sems["s"], cnt["s"])
                    li = scalar.activation(s_st2[:], s_st2[:], AF.Exp, scale=-0.5)
                    li.then_inc(sems["s"], 1)
                on("scalar", f)
                cnt["s"] += 1
                sval = cnt["s"]
                dbg("rstd_c%d" % g, s_st2[:])

                def f(vector, cc=cc, sval=sval):
                    w(vector, "s", sval)
                    li = vector.tensor_mul(s_ep1[:].rearrange("p (h j) -> p h j", h=H),
                                      s_ep1[:].rearrange("p (h j) -> p h j", h=H),
                                      s_st2[:].rearrange("p (h o) -> p h o", o=1).broadcast_to([128, H, HK]))
                    li = vector.tensor_mul(s_ep1[:], s_ep1[:], s_gb[:, 0, :])
                    li = vector.tensor_add(s_ep1[:], s_ep1[:], s_gb[:, 1, :])
                    li = vector.tensor_mul(s_zg[:], s_ep1[:], s_gg[:, cc, :])
                    li.then_inc(sems["v"], 1)
                on("vector", f)
                cnt["v"] += 1
                v_zg = cnt["v"]
                dbg("zg1_c%d" % g, s_zg[:])

                for half in range(2):
                    def f(tensor, half=half, v_zg=v_zg):
                        w(tensor, "v", v_zg)
                        w(tensor, "s")
                        for i in range(4):
                            s = half * 4 + i
                            li = tensor.transpose(p_T[:, i * 128:(i + 1) * 128],
                                             s_zg[:, 128 * s:128 * (s + 1)], s_id[:])
                        li.then_inc(sems["t"], 1)
                    on("tensor", f)
                    cnt["t"] += 1
                    tval = cnt["t"]

                    def f(scalar, half=half, tval=tval):
                        w(scalar, "t", tval)
                        li = scalar.copy(s_zf[:, half * 4:(half + 1) * 4, :],
                                    p_T[:].rearrange("p (s c) -> p s c", s=4))
                        li.then_inc(sems["s"], 1)
                    on("scalar", f)
                    cnt["s"] += 1
                s_zfd = cnt["s"]

                def f(tensor, s_zfd=s_zfd, ws_o=ws_o):
                    w(tensor, "s", s_zfd)
                    w(tensor, "ws", ws_o)
                    w(tensor, "v")
                    for dc in range(SL):
                        for hh in range(2):
                            li = tensor.matmul(p_Z[:, hh * 512:(hh + 1) * 512], s_zf[:, dc, :],
                                          s_ws[:, dc, hh * 512:(hh + 1) * 512],
                                          start=(dc == 0), stop=(dc == SL - 1))
                    li.then_inc(sems["t"], 1)
                on("tensor", f)
                cnt["t"] += 1
                t_att = cnt["t"]

                def f(vector, cc=cc, t_att=t_att, xe_w=xe_w, ename=ename, epar=epar):
                    w(vector, "t", t_att)
                    w(vector, ename, xe_w)
                    li = vector.tensor_add(s_ep1[:], p_Z[:], s_xep[:, epar, :])
                    li.then_inc(sems["v"], 1)
                on("vector", f)
                cnt["v"] += 1
                vval = cnt["v"]
                dbg("res_c%d" % g, s_ep1[:])

                def f(scalar, vval=vval):
                    w(scalar, "v", vval)
                    li = scalar.activation(s_ep2[:], s_ep1[:], AF.Identity, accum_out=s_st3[:, 0:1])
                    li.then_inc(sems["s"], 1)
                on("scalar", f)
                cnt["s"] += 1
                sval = cnt["s"]

                def f(vector, sval=sval):
                    w(vector, "s", sval)
                    li = vector.tensor_scalar_mul(s_st3[:, 0:1], s_st3[:, 0:1], 1.0 / C)
                    li.then_inc(sems["v"], 1)
                    cnt["v"] += 1
                    vector.wait_ge(sems["v"], cnt["v"])
                    li = vector.tensor_scalar_sub(s_ep1[:], s_ep1[:], s_st3[:, 0:1])
                    li.then_inc(sems["v"], 1)
                on("vector", f)
                cnt["v"] += 1
                vval = cnt["v"]

                def f(scalar, vval=vval):
                    w(scalar, "v", vval)
                    li = scalar.activation(s_ep2[:], s_ep1[:], AF.Square, accum_out=s_st3[:, 1:2])
                    li.then_inc(sems["s"], 1)
                on("scalar", f)
                cnt["s"] += 1
                sval = cnt["s"]

                def f(vector, sval=sval):
                    w(vector, "s", sval)
                    li = vector.tensor_scalar(s_st3[:, 1:2], s_st3[:, 1:2], 1.0 / C, EPS_LN, ALU.mult, ALU.add)
                    li.then_inc(sems["v"], 1)
                on("vector", f)
                cnt["v"] += 1
                vval = cnt["v"]

                def f(scalar, vval=vval):
                    w(scalar, "v", vval)
                    li = scalar.activation(s_st3[:, 1:2], s_st3[:, 1:2], AF.Ln)
                    li.then_inc(sems["s"], 1)
                    cnt["s"] += 1
                    scalar.wait_ge(sems["s"], cnt["s"])
                    li = scalar.activation(s_st3[:, 1:2], s_st3[:, 1:2], AF.Exp, scale=-0.5)
                    li.then_inc(sems["s"], 1)
                on("scalar", f)
                cnt["s"] += 1
                sval = cnt["s"]

                def f(vector, sval=sval):
                    w(vector, "s", sval)
                    li = vector.tensor_scalar_mul(s_zg[:], s_ep1[:], s_st3[:, 1:2])
                    li.then_inc(sems["v"], 1)
                on("vector", f)
                cnt["v"] += 1
                v_x1 = cnt["v"]

                tok0 = (m - n_mac_pre) * MAC + cc * CH
                for half in range(2):
                    def f(tensor, half=half, v_x1=v_x1):
                        w(tensor, "v", v_x1)
                        w(tensor, "s")
                        for i in range(4):
                            s = half * 4 + i
                            li = tensor.transpose(p_T[:, i * 128:(i + 1) * 128],
                                             s_zg[:, 128 * s:128 * (s + 1)], s_id[:])
                        li.then_inc(sems["t"], 1)
                    on("tensor", f)
                    cnt["t"] += 1
                    tval = cnt["t"]

                    def f(scalar, half=half, tval=tval, tok0=tok0):
                        w(scalar, "t", tval)
                        li = scalar.copy(s_x1[:, half * 4:(half + 1) * 4, tok0:tok0 + CH],
                                    p_T[:].rearrange("p (s c) -> p s c", s=4))
                        li.then_inc(sems["s"], 1)
                    on("scalar", f)
                    cnt["s"] += 1

                dbg("x1_c%d" % g, s_x1[:, :, tok0:tok0 + CH])

        dbg("x1", s_x1[:])

        inner.close()

        if do_ffn:
            NT128 = (n_mac_own * MAC) // 128
            NT512 = (n_mac_own * MAC) // 512
            s_acc = E(nc.sbuf_tensor("s_acc", [128, NT128, C], F32))
            s_wu = E(nc.sbuf_tensor("s_wu", [128, 2, SL, 1024], BF16))
            s_wd2 = E(nc.sbuf_tensor("s_wd2", [128, 2, SL, C], BF16))
            s_h = E(nc.sbuf_tensor("s_h", [128, SL, 512], BF16))
            s_cmb = E(nc.sbuf_tensor("s_cmb", [128, NT128, 8], F32))
            s_lg = E(nc.sbuf_tensor("s_lg", [128, NT128, 8], F32))
            s_q = E(nc.sbuf_tensor("s_q", [128, NT128, 8], F32))
            s_r8 = E(nc.sbuf_tensor("s_r8", [128, NT128, 4], F32))
            s_lgf = E(nc.sbuf_tensor("s_lgf", [8, 512], F32))
            s_rw = E(nc.sbuf_tensor("s_rw", [128, SL, 8], BF16))
            s_rb = E(nc.sbuf_tensor("s_rb", [8, 1], F32))
            s_if = E(nc.sbuf_tensor("s_if", [128, 128], F32))
            s_out = E(nc.sbuf_tensor("s_out", [128, 2, C], F32))

            bar = {k: cnt[k] for k in ["t", "v", "s", "g"]}
            for eng in ["tensor", "vector", "scalar", "gpsimd", "sync"]:
                def f(e, bar=bar):
                    for k, vv in bar.items():
                        w(e, k, vv)
                on(eng, f)

            def f(sync):
                li = sync.dma_start(out=s_rb[:], in_=d_rb).then_inc(sems["c0"], 16)
                li = sync.dma_start(out=s_if[:], in_=i_idf.ap()).then_inc(sems["c0"], 16)
            on("sync", f)
            cnt["c0"] += 32

            def f(gpsimd):
                li = gpsimd.dma_start(out=s_rw[:], in_=d_rw.rearrange("s p c -> p s c")).then_inc(sems["wd"], 16)
            on("gpsimd", f)
            cnt["wd"] += 16
            x0_ffn = cnt["c0"]
            wd_ffn = cnt["wd"]

            def f(vector):
                w(vector, "c0", x0_ffn)
                w(vector, "wd", wd_ffn)
                li = vector.memset(s_acc[:], 0.0)
                li.then_inc(sems["v"], 1)
            on("vector", f)
            cnt["v"] += 1

            for tt in range(NT512):
                def f(tensor, tt=tt):
                    w(tensor, "v")
                    for dc in range(SL):
                        li = tensor.matmul(p_A[0:8, 0:512], s_rw[:, dc, :],
                                      s_x1[:, dc, tt * 512:(tt + 1) * 512],
                                      start=(dc == 0), stop=(dc == SL - 1))
                    li.then_inc(sems["t"], 1)
                on("tensor", f)
                cnt["t"] += 1
                tval = cnt["t"]

                def f(vector, tval=tval):
                    w(vector, "t", tval)
                    li = vector.tensor_scalar_add(s_lgf[:], p_A[0:8, 0:512], s_rb[:])
                    li.then_inc(sems["v"], 1)
                on("vector", f)
                cnt["v"] += 1
                vval = cnt["v"]

                def f(tensor, vval=vval):
                    w(tensor, "v", vval)
                    for i in range(4):
                        li = tensor.transpose(p_B[:, i * 8:(i + 1) * 8],
                                         s_lgf[:, i * 128:(i + 1) * 128], s_if[0:8, 0:8])
                    li.then_inc(sems["t"], 1)
                on("tensor", f)
                cnt["t"] += 1
                tval = cnt["t"]

                def f(vector, tt=tt, tval=tval):
                    w(vector, "t", tval)
                    li = vector.tensor_copy(s_lg[:, tt * 4:(tt + 1) * 4, :],
                                       p_B[:, 0:32].rearrange("p (c e) -> p c e", e=8))
                    li.then_inc(sems["v"], 1)
                on("vector", f)
                cnt["v"] += 1

            def f(vector):
                li = vector.tensor_reduce(s_r8[:, :, 0:1], s_lg[:], AXX, ALU.max)
                li.then_inc(sems["v"], 1)
                cnt["v"] += 1
                vector.wait_ge(sems["v"], cnt["v"])
                li = vector.tensor_sub(s_q[:], s_lg[:], s_r8[:, :, 0:1].broadcast_to([128, NT128, 8]))
                li.then_inc(sems["v"], 1)
            on("vector", f)
            cnt["v"] += 1
            vval = cnt["v"]

            def f(scalar, vval=vval):
                w(scalar, "v", vval)
                li = scalar.activation(s_q[:], s_q[:], AF.Exp)
                li.then_inc(sems["s"], 1)
            on("scalar", f)
            cnt["s"] += 1
            sval = cnt["s"]

            def f(vector, sval=sval):
                w(vector, "s", sval)
                li = vector.tensor_single_scalar(s_cmb[:], s_q[:], 1.0, ALU.is_lt)
                li.then_inc(sems["v"], 1)
                cnt["v"] += 1
                vector.wait_ge(sems["v"], cnt["v"])
                li = vector.tensor_mul(s_cmb[:], s_cmb[:], s_q[:])
                li.then_inc(sems["v"], 1)
                cnt["v"] += 1
                vector.wait_ge(sems["v"], cnt["v"])
                li = vector.tensor_reduce(s_r8[:, :, 1:2], s_cmb[:], AXX, ALU.max)
                li.then_inc(sems["v"], 1)
                cnt["v"] += 1
                vector.wait_ge(sems["v"], cnt["v"])
                li = vector.tensor_scalar_add(s_r8[:, :, 2:3], s_r8[:, :, 1:2], 1.0)
                li.then_inc(sems["v"], 1)
                cnt["v"] += 1
                vector.wait_ge(sems["v"], cnt["v"])
                li = vector.reciprocal(s_r8[:, :, 3:4], s_r8[:, :, 2:3])
                li.then_inc(sems["v"], 1)
                cnt["v"] += 1
                vector.wait_ge(sems["v"], cnt["v"])
                li = vector.tensor_tensor(s_lg[:], s_q[:], s_r8[:, :, 1:2].broadcast_to([128, NT128, 8]), ALU.is_ge)
                li.then_inc(sems["v"], 1)
                cnt["v"] += 1
                vector.wait_ge(sems["v"], cnt["v"])
                li = vector.tensor_mul(s_q[:], s_q[:], s_lg[:])
                li.then_inc(sems["v"], 1)
                cnt["v"] += 1
                vector.wait_ge(sems["v"], cnt["v"])
                li = vector.tensor_mul(s_cmb[:], s_q[:], s_r8[:, :, 3:4].broadcast_to([128, NT128, 8]))
                li.then_inc(sems["v"], 1)
            on("vector", f)
            cnt["v"] += 1
            dbg("combine", s_cmb[:])

            tdone = []
            for eidx in range(NE):
                for fh in range(2):
                    idx = eidx * 2 + fh
                    ring = idx % 2
                    wsem = "w%d" % ring

                    def f(gpsimd, eidx=eidx, fh=fh, ring=ring, wsem=wsem, idx=idx):
                        if idx >= 2:
                            w(gpsimd, "t", tdone[idx - 2])
                        li = gpsimd.dma_start(out=s_wu[:, ring, :, :],
                                         in_=d_up[eidx, :, :, fh * 1024:(fh + 1) * 1024].rearrange("s p f -> p s f")
                                         ).then_inc(sems[wsem], 16)
                        li = gpsimd.dma_start(out=s_wd2[:, ring, :, :],
                                         in_=d_dn[eidx, fh * 8:(fh + 1) * 8, :, :].rearrange("s p f -> p s f")
                                         ).then_inc(sems[wsem], 16)
                    on("gpsimd", f)
                    cnt[wsem] += 32
                    wv = cnt[wsem]

                    for tt in range(NT512):
                        for fc in range(SL):
                            pp = p_A if fc % 2 == 0 else p_B

                            def f(tensor, ring=ring, tt=tt, fc=fc, wsem=wsem, wv=wv, pp=pp):
                                if fc == 0 and tt == 0:
                                    w(tensor, wsem, wv)
                                w(tensor, "s")
                                for dc in range(SL):
                                    li = tensor.matmul(pp[:, 0:512], s_wu[:, ring, dc, fc * 128:(fc + 1) * 128],
                                                  s_x1[:, dc, tt * 512:(tt + 1) * 512],
                                                  start=(dc == 0), stop=(dc == SL - 1))
                                li.then_inc(sems["t"], 1)
                            on("tensor", f)
                            cnt["t"] += 1
                            tval = cnt["t"]

                            def f(scalar, fc=fc, tval=tval, pp=pp):
                                w(scalar, "t", tval)
                                li = scalar.activation(s_h[:, fc, :], pp[:, 0:512], AF.Silu)
                                li.then_inc(sems["s"], 1)
                            on("scalar", f)
                            cnt["s"] += 1
                        s_hd = cnt["s"]

                        for st in range(4):
                            t128 = tt * 4 + st
                            pd = p_Y if st % 2 == 0 else p_Z

                            def f(tensor, ring=ring, st=st, s_hd=s_hd, pd=pd):
                                w(tensor, "s", s_hd)
                                w(tensor, "v")
                                for fc in range(SL):
                                    for hh in range(2):
                                        li = tensor.matmul(pd[:, hh * 512:(hh + 1) * 512],
                                                      s_h[:, fc, st * 128:(st + 1) * 128],
                                                      s_wd2[:, ring, fc, hh * 512:(hh + 1) * 512],
                                                      start=(fc == 0), stop=(fc == SL - 1))
                                li.then_inc(sems["t"], 1)
                            on("tensor", f)
                            cnt["t"] += 1
                            tval = cnt["t"]

                            def f(vector, eidx=eidx, t128=t128, tval=tval, pd=pd):
                                w(vector, "t", tval)
                                if eidx < 8:
                                    li = vector.scalar_tensor_tensor(s_acc[:, t128, :], pd[:],
                                                                s_cmb[:, t128, eidx:eidx + 1],
                                                                s_acc[:, t128, :], ALU.mult, ALU.add)
                                else:
                                    li = vector.tensor_add(s_acc[:, t128, :], pd[:], s_acc[:, t128, :])
                                li.then_inc(sems["v"], 1)
                            on("vector", f)
                            cnt["v"] += 1
                    tdone.append(cnt["t"])

            dbg("ffnacc", s_acc[:, 0, :])
            for t128 in range(NT128):
                for half in range(2):
                    def f(tensor, t128=t128, half=half):
                        w(tensor, "v")
                        w(tensor, "s")
                        for i in range(4):
                            s = half * 4 + i
                            li = tensor.transpose(p_T[:, i * 128:(i + 1) * 128],
                                             s_x1[:, s, t128 * 128:(t128 + 1) * 128], s_id[:])
                        li.then_inc(sems["t"], 1)
                    on("tensor", f)
                    cnt["t"] += 1
                    tval = cnt["t"]

                    def f(vector, half=half, tval=tval):
                        w(vector, "t", tval)
                        li = vector.tensor_copy(s_zg[:, half * 512:(half + 1) * 512], p_T[:])
                        li.then_inc(sems["v"], 1)
                    on("vector", f)
                    cnt["v"] += 1

                if t128 == 0:
                    dbg("x1tm0", s_zg[:])

                if t128 == 0:
                    dbg("fzg0", s_zg[:])

                def f(vector, t128=t128):
                    w(vector, "s")
                    li = vector.tensor_add(s_ep1[:], s_acc[:, t128, :], s_zg[:])
                    li.then_inc(sems["v"], 1)
                on("vector", f)
                cnt["v"] += 1
                vval = cnt["v"]
                if t128 == 0:
                    dbg("fres0", s_ep1[:])
                    dbg("facc0", s_acc[:, 0, :])

                def f(scalar, vval=vval):
                    w(scalar, "v", vval)
                    w(scalar, "st")
                    li = scalar.activation(s_ep2[:], s_ep1[:], AF.Identity, accum_out=s_st3[:, 0:1])
                    li.then_inc(sems["s"], 1)
                on("scalar", f)
                cnt["s"] += 1
                sval = cnt["s"]

                def f(vector, sval=sval):
                    w(vector, "s", sval)
                    li = vector.tensor_scalar_mul(s_st3[:, 0:1], s_st3[:, 0:1], 1.0 / C)
                    li.then_inc(sems["v"], 1)
                    cnt["v"] += 1
                    vector.wait_ge(sems["v"], cnt["v"])
                    li = vector.tensor_scalar_sub(s_ep1[:], s_ep1[:], s_st3[:, 0:1])
                    li.then_inc(sems["v"], 1)
                on("vector", f)
                cnt["v"] += 1
                vval = cnt["v"]

                def f(scalar, vval=vval):
                    w(scalar, "v", vval)
                    li = scalar.activation(s_ep2[:], s_ep1[:], AF.Square, accum_out=s_st3[:, 1:2])
                    li.then_inc(sems["s"], 1)
                on("scalar", f)
                cnt["s"] += 1
                sval = cnt["s"]

                def f(vector, sval=sval):
                    w(vector, "s", sval)
                    li = vector.tensor_scalar(s_st3[:, 1:2], s_st3[:, 1:2], 1.0 / C, EPS_LN, ALU.mult, ALU.add)
                    li.then_inc(sems["v"], 1)
                on("vector", f)
                cnt["v"] += 1
                vval = cnt["v"]

                def f(scalar, vval=vval):
                    w(scalar, "v", vval)
                    li = scalar.activation(s_st3[:, 1:2], s_st3[:, 1:2], AF.Ln)
                    li.then_inc(sems["s"], 1)
                    cnt["s"] += 1
                    scalar.wait_ge(sems["s"], cnt["s"])
                    li = scalar.activation(s_st3[:, 1:2], s_st3[:, 1:2], AF.Exp, scale=-0.5)
                    li.then_inc(sems["s"], 1)
                on("scalar", f)
                cnt["s"] += 1
                sval = cnt["s"]

                def f(vector, t128=t128, sval=sval):
                    w(vector, "s", sval)
                    li = vector.tensor_scalar_mul(s_out[:, t128 % 2, :], s_ep1[:], s_st3[:, 1:2])
                    li.then_inc(sems["v"], 1)
                on("vector", f)
                cnt["v"] += 1
                vval = cnt["v"]
                if t128 == 0:
                    dbg("fout0", s_out[:, 0, :])
                    dbg("fst3", s_st3[:])

                def f(sync, t128=t128, vval=vval):
                    w(sync, "v", vval)
                    li = sync.dma_start(out=d_out[t128 * 128:(t128 + 1) * 128, :],
                                   in_=s_out[:, t128 % 2, :]).then_inc(sems["st"], 16)
                on("sync", f)
                cnt["st"] += 16

        def f(sync):
            w(sync, "st")
        on("sync", f)

        def f(gpsimd):
            w(gpsimd, "db")
        on("gpsimd", f)

    return nc



def _prep_shared(inputs):
    A = lambda k: np.asarray(inputs[k], dtype=np.float32)
    shared = {}
    for nm, key in [("W_r", "L0_attn_W_r"), ("W_k", "L0_attn_W_k"), ("W_v", "L0_attn_W_v"),
                    ("W_g", "L0_attn_W_g"), ("W_o", "L0_attn_W_o")]:
        shared[nm] = _mk(A(key).reshape(SL, 128, C))
    shared["maa_w1"] = _mk(A("L0_attn_time_maa_w1").reshape(SL, 128, 160))
    tm = [A("L0_attn_time_maa_w"), A("L0_attn_time_maa_k"), A("L0_attn_time_maa_v"),
          A("L0_attn_time_maa_r"), A("L0_attn_time_maa_g")]
    w2 = A("L0_attn_time_maa_w2")
    shared["w2aug"] = _mk(np.concatenate([w2, np.stack(tm)[:, None, :]], axis=1).transpose(1, 0, 2))
    shared["dec_w1"] = _mk(A("L0_attn_time_decay_w1").reshape(SL, 128, 64))
    shared["dec_w2"] = _mk(A("L0_attn_time_decay_w2"))
    tvec = np.stack([A("L0_attn_time_maa_x"), A("L0_attn_time_decay"),
                     A("L0_attn_time_faaaa").reshape(-1)], axis=-1)
    shared["tvec"] = _mk(tvec.reshape(SL, 128, 3).transpose(1, 0, 2))
    gb = np.stack([A("L0_attn_ln_gamma"), A("L0_attn_ln_beta")])
    shared["lngb"] = _mk(np.broadcast_to(gb[None], (128, 2, C)))
    shared["router_wT"] = _mk(A("L0_ffn_router_w").T.reshape(SL, 128, 8))
    shared["router_b"] = _mk(A("L0_ffn_router_b").reshape(8, 1))
    up = np.concatenate([A("L0_ffn_w_up"), A("L0_ffn_shared_w_up")], axis=0)
    dn = np.concatenate([A("L0_ffn_w_down"), A("L0_ffn_shared_w_down")], axis=0)
    shared["wupT"] = _mk(up.transpose(0, 2, 1).reshape(NE, SL, 128, F_FF))
    shared["wdnT"] = _mk(dn.transpose(0, 2, 1).reshape(NE, 16, 128, C))
    return shared


def _core_x(xfull, core, t_win):
    s, p = core // 4, core % 4
    t0 = p * T_OWN
    lo = t0 + T_OWN - (t_win + 1)
    seq = xfull[s]
    if lo >= 0:
        return _mk(seq[lo:t0 + T_OWN].astype(np.float32))
    pad = np.zeros((-lo, C), np.float32)
    return _mk(np.concatenate([pad, seq[0:t0 + T_OWN]]).astype(np.float32))


def kernel(**inputs):
    shared = _prep_shared(inputs)
    xfull = np.asarray(inputs["L0_attn_x"], dtype=np.float32)
    n_pre, n_own = _N_PRE, _N_OWN
    t_win = (n_pre + n_own) * MAC
    nc = build_program(n_mac_pre=n_pre, n_mac_own=n_own, do_ffn=_DO_FFN, debug=_DEBUG)
    in_maps = []
    for core in range(N_CORE):
        mm = dict(shared)
        mm["xin"] = _core_x(xfull, core, t_win)
        in_maps.append(mm)
    res = run_bass_kernel_spmd(nc, in_maps, core_ids=list(range(N_CORE)))
    _LAST_RES[0] = res
    out = np.zeros((2, 8192, C), np.float32)
    for core in range(N_CORE):
        s, p = core // 4, core % 4
        out[s, p * T_OWN:(p + 1) * T_OWN] = np.asarray(res.results[core]["out"], dtype=np.float32)
    return out


# version 02
# speedup vs baseline: 1.1647x; 1.1647x over previous
import contextlib
import numpy as np
import ml_dtypes
import concourse.bass as bass
import concourse.mybir as mybir
from concourse.bass_utils import run_bass_kernel_spmd

F32 = mybir.dt.float32
BF16 = mybir.dt.bfloat16
AF = mybir.ActivationFunctionType
ALU = mybir.AluOpType
AXX = mybir.AxisListType.X
BF = ml_dtypes.bfloat16

C = 1024
SL = 8
H = 16
HK = 64
CH = 128
MAC = 512
CPM = MAC // CH
T_OWN = 2048
N_CORE = 8
EPS_GN = 1e-5 * 64.0
EPS_LN = 1e-5
NE = 9
F_FF = 2048

_DEBUG = []
_MAX_ON = None
_Y_TERMS = 3
_ON_TRACE = []
_N_PRE = 12
_N_OWN = 4
_DO_FFN = True
_LAST_RES = [None]


def _mk(x):
    return np.ascontiguousarray(x)


def build_program(n_mac_pre=12, n_mac_own=4, do_ffn=True, debug=()):
    nc = bass.Bass("TRN2", target_bir_lowering=False, detect_race_conditions=False)
    NMAC = n_mac_pre + n_mac_own
    NCH = NMAC * CPM
    T_WIN = NMAC * MAC

    d_x = nc.dram_tensor("xin", [T_WIN + 1, C], F32, kind="ExternalInput").ap()
    d_wk = nc.dram_tensor("W_k", [SL, 128, C], F32, kind="ExternalInput").ap()
    d_wv = nc.dram_tensor("W_v", [SL, 128, C], F32, kind="ExternalInput").ap()
    d_wr = nc.dram_tensor("W_r", [SL, 128, C], F32, kind="ExternalInput").ap()
    d_wg = nc.dram_tensor("W_g", [SL, 128, C], F32, kind="ExternalInput").ap()
    d_wo = nc.dram_tensor("W_o", [SL, 128, C], F32, kind="ExternalInput").ap()
    d_w1 = nc.dram_tensor("maa_w1", [SL, 128, 160], F32, kind="ExternalInput").ap()
    d_w2 = nc.dram_tensor("w2aug", [33, 5, C], F32, kind="ExternalInput").ap()
    d_d1 = nc.dram_tensor("dec_w1", [SL, 128, 64], F32, kind="ExternalInput").ap()
    d_d2 = nc.dram_tensor("dec_w2", [64, C], F32, kind="ExternalInput").ap()
    d_tv = nc.dram_tensor("tvec", [128, SL, 3], F32, kind="ExternalInput").ap()
    d_gb = nc.dram_tensor("lngb", [128, 2, C], F32, kind="ExternalInput").ap()
    d_rw = nc.dram_tensor("router_wT", [SL, 128, 8], F32, kind="ExternalInput").ap()
    d_rb = nc.dram_tensor("router_b", [8, 1], F32, kind="ExternalInput").ap()
    d_up = nc.dram_tensor("wupT", [NE, SL, 128, F_FF], F32, kind="ExternalInput").ap()
    d_dn = nc.dram_tensor("wdnT", [NE, 16, 128, C], F32, kind="ExternalInput").ap()
    d_out = nc.dram_tensor("out", [T_OWN, C], F32, kind="ExternalOutput").ap()

    dbg_out = {}
    for name, shape in debug:
        dbg_out[name] = nc.dram_tensor("dbg_" + name, list(shape), F32, kind="ExternalOutput").ap()

    i_idb = nc.inline_tensor(np.eye(128, dtype=BF), "idb")
    i_idf = nc.inline_tensor(np.eye(128, dtype=np.float32), "idf")
    i_mask = nc.inline_tensor(np.triu(np.ones((128, 128), dtype=BF), 1), "masksu")
    i_ones = nc.inline_tensor(np.ones((128, 64), dtype=BF), "ones64")

    with contextlib.ExitStack() as ctx:
        E = ctx.enter_context

        sems, cnt = {}, {}
        for s in ["c0", "x0", "x1", "xe0", "xe1", "wd", "ws", "t", "v", "s", "g", "st", "xf", "w0", "w1", "db"]:
            sems[s] = E(nc.semaphore("sem_" + s))
            cnt[s] = 0

        def w(eng, name, val=None):
            v = cnt[name] if val is None else val
            assert v <= cnt[name], ("forward wait", name, v, cnt[name])
            eng.wait_ge(sems[name], v)

        s_x1 = E(nc.sbuf_tensor("s_x1", [128, SL, T_OWN], BF16))
        s_id = E(nc.sbuf_tensor("s_id", [128, 128], BF16))
        s_mkk = E(nc.sbuf_tensor("s_mkk", [128, 128], BF16))
        s_o64 = E(nc.sbuf_tensor("s_o64", [128, 64], BF16))
        s_ep1 = E(nc.sbuf_tensor("s_ep1", [128, C], F32))
        s_ep2 = E(nc.sbuf_tensor("s_ep2", [128, C], BF16))
        s_zg = E(nc.sbuf_tensor("s_zg", [128, C], BF16))
        s_st3 = E(nc.sbuf_tensor("s_st3", [128, 4], F32))

        p_A = E(nc.psum_tensor("p_A", [128, 512], F32))
        p_B = E(nc.psum_tensor("p_B", [128, 512], F32))
        p_Y = E(nc.psum_tensor("p_Y", [128, 1024], F32))
        p_Z = E(nc.psum_tensor("p_Z", [128, 1024], F32))
        p_T = E(nc.psum_tensor("p_T", [128, 512], BF16))

        block = E(nc.Block())

        _onctr = [0]

        def on(engname, f):
            import sys as _sys
            _onctr[0] += 1
            _ON_TRACE.append((_onctr[0], engname, _sys._getframe(1).f_lineno))
            if _MAX_ON is not None and _onctr[0] > _MAX_ON:
                return
            getattr(block, engname)(f)

        inner = ctx.enter_context(contextlib.ExitStack())
        E2 = inner.enter_context

        s_wk = E2(nc.sbuf_tensor("s_wk", [128, SL, C], BF16))
        s_wv = E2(nc.sbuf_tensor("s_wv", [128, SL, C], BF16))
        s_ws = E2(nc.sbuf_tensor("s_ws", [128, SL, C], BF16))
        s_w1 = E2(nc.sbuf_tensor("s_w1", [128, SL, 160], BF16))
        s_w2 = E2(nc.sbuf_tensor("s_w2", [33, 5, C], BF16))
        s_d1 = E2(nc.sbuf_tensor("s_d1", [128, SL, 64], BF16))
        s_d2 = E2(nc.sbuf_tensor("s_d2", [64, C], BF16))
        s_tv = E2(nc.sbuf_tensor("s_tv", [128, SL, 3], F32))
        s_gb = E2(nc.sbuf_tensor("s_gb", [128, 2, C], BF16))
        s_n1 = E2(nc.sbuf_tensor("s_n1", [128, CH], F32))
        s_S = E2(nc.sbuf_tensor("s_S", [128, SL, HK], F32))
        s_Sc = E2(nc.sbuf_tensor("s_Sc", [128, SL, HK], BF16))

        s_xf = E2(nc.sbuf_tensor("s_xf", [128, SL, MAC], BF16))
        s_xx = E2(nc.sbuf_tensor("s_xx", [128, SL, MAC], BF16))
        s_mx = E2(nc.sbuf_tensor("s_mx", [128, SL, MAC], BF16))
        s_rf = E2(nc.sbuf_tensor("s_rf", [128, SL, MAC], BF16))
        s_kf = E2(nc.sbuf_tensor("s_kf", [128, SL, MAC], BF16))
        s_vt = E2(nc.sbuf_tensor("s_vt", [128, CPM, C], BF16))
        s_gg = E2(nc.sbuf_tensor("s_gg", [128, CPM, C], BF16))
        s_m5 = E2(nc.sbuf_tensor("s_m5", [33, 5, MAC], BF16))
        s_m6 = E2(nc.sbuf_tensor("s_m6", [64, MAC], BF16))
        s_pc = E2(nc.sbuf_tensor("s_pc", [128, SL, 1], BF16))

        s_xtm = E2(nc.sbuf_tensor("s_xtm", [128, 2, C], BF16))
        s_xep = E2(nc.sbuf_tensor("s_xep", [128, 2, C], BF16))
        s_ew = E2(nc.sbuf_tensor("s_ew", [128, SL, CH], F32))
        s_L = E2(nc.sbuf_tensor("s_L", [128, SL, CH + 1], F32))
        s_D1 = E2(nc.sbuf_tensor("s_D1", [128, SL, CH], BF16))
        s_D2 = E2(nc.sbuf_tensor("s_D2", [128, SL, CH], BF16))
        s_Rp = E2(nc.sbuf_tensor("s_Rp", [128, SL, CH], BF16))
        s_Kp = E2(nc.sbuf_tensor("s_Kp", [128, SL, CH], BF16))
        s_Kt = E2(nc.sbuf_tensor("s_Kt", [128, SL, CH], BF16))
        s_P2 = E2(nc.sbuf_tensor("s_P2", [128, SL, CH], BF16))
        s_M = E2(nc.sbuf_tensor("s_M", [128, 4, CH], BF16))
        s_ti = E2(nc.sbuf_tensor("s_ti", [128, SL, 2], F32))
        s_te = E2(nc.sbuf_tensor("s_te", [128, SL, 2], F32))
        s_zf = E2(nc.sbuf_tensor("s_zf", [128, SL, 128], BF16))
        s_st1 = E2(nc.sbuf_tensor("s_st1", [128, H], F32))
        s_st2 = E2(nc.sbuf_tensor("s_st2", [128, H], F32))
        s_dg = E2(nc.sbuf_tensor("s_dg", [128, H], F32))

        def f(sync):
            li = sync.dma_start(out=s_tv[:], in_=d_tv).then_inc(sems["c0"], 16)
            li = sync.dma_start(out=s_id[:], in_=i_idb.ap()).then_inc(sems["c0"], 16)
            li = sync.dma_start(out=s_mkk[:], in_=i_mask.ap()).then_inc(sems["c0"], 16)
            li = sync.dma_start(out=s_o64[:], in_=i_ones.ap()).then_inc(sems["c0"], 16)
        on("sync", f)
        cnt["c0"] += 4 * 16

        def f(gpsimd):
            for dst, src in [(s_wk, d_wk), (s_wv, d_wv), (s_w1, d_w1), (s_d1, d_d1)]:
                li = gpsimd.dma_start(out=dst[:], in_=src.rearrange("s p c -> p s c")).then_inc(sems["wd"], 16)
            li = gpsimd.dma_start(out=s_w2[:], in_=d_w2).then_inc(sems["wd"], 16)
            li = gpsimd.dma_start(out=s_d2[:], in_=d_d2).then_inc(sems["wd"], 16)
            li = gpsimd.dma_start(out=s_gb[:], in_=d_gb).then_inc(sems["wd"], 16)
            li = gpsimd.dma_start(out=s_pc[:], in_=d_x[0:1, :].rearrange("r (s p) -> p s r", p=128),
                             allow_slow_non_contiguous=True).then_inc(sems["wd"], 16)
        on("gpsimd", f)
        cnt["wd"] += 8 * 16

        def f(vector):
            w(vector, "c0")
            w(vector, "wd")
            li = vector.memset(s_n1[:], -1.0)
            li = vector.memset(s_S[:], 0.0)
            li = vector.memset(s_L[:, :, 0:1], 0.0)
            li = vector.memset(s_m5[32:33, :, :], 1.0)
            li.then_inc(sems["v"], 1)
        on("vector", f)
        cnt["v"] += 1
        v_init = cnt["v"]

        xw_after = {}

        def emit_xload(g):
            par = g % 2
            sname = "x%d" % par

            def f(gpsimd, g=g, par=par, sname=sname):
                if g >= 2:
                    w(gpsimd, "xf", g - 1)
                li = gpsimd.dma_start(out=s_xtm[:, par, :],
                                 in_=d_x[1 + g * CH: 1 + (g + 1) * CH, :]).then_inc(sems[sname], 16)
            on("gpsimd", f)
            cnt[sname] += 16
            xw_after[g] = cnt[sname]

        def dbg(name, ap_src):
            if name in dbg_out:
                def f(gpsimd):
                    for sn in ["t", "v", "s", "g"]:
                        w(gpsimd, sn)
                    li = gpsimd.dma_start(out=dbg_out[name], in_=ap_src).then_inc(sems["db"], 16)
                on("gpsimd", f)
                cnt["db"] += 16
                for eng in ["tensor", "vector", "scalar"]:
                    def f(e):
                        w(e, "db")
                    on(eng, f)

        emit_xload(0)
        emit_xload(1)

        TGT = ["w", "k", "v", "r", "g"]

        for m in range(NMAC):
            own = m >= n_mac_pre
            tgts = TGT if own else TGT[:3]

            if own:
                def f(gpsimd):
                    w(gpsimd, "t")
                    li = gpsimd.dma_start(out=s_ws[:], in_=d_wr.rearrange("s p c -> p s c")).then_inc(sems["ws"], 16)
                on("gpsimd", f)
                cnt["ws"] += 16
            ws_r = cnt["ws"]

            for cc in range(CPM):
                g = m * CPM + cc
                par = g % 2
                xname = "x%d" % par
                for half in range(2):
                    def f(tensor, g=g, par=par, half=half, xname=xname, m=m, cc=cc):
                        if m == 0 and cc == 0 and half == 0:
                            w(tensor, "v", v_init)
                        w(tensor, xname, xw_after[g])
                        w(tensor, "v")
                        if half == 0 and cc == 0:
                            w(tensor, "v")
                            w(tensor, "g")
                        for i in range(4):
                            s = half * 4 + i
                            li = tensor.transpose(p_T[:, i * 128:(i + 1) * 128],
                                             s_xtm[:, par, 128 * s:128 * (s + 1)], s_id[:])
                        li.then_inc(sems["t"], 1)
                    on("tensor", f)
                    cnt["t"] += 1
                    tval = cnt["t"]

                    def f(vector, cc=cc, half=half, tval=tval):
                        w(vector, "t", tval)
                        li = vector.tensor_copy(s_xf[:, half * 4:(half + 1) * 4, cc * CH:(cc + 1) * CH],
                                    p_T[:].rearrange("p (s c) -> p s c", s=4))
                        li.then_inc(sems["v"], 1)
                    on("vector", f)
                    cnt["v"] += 1
                s_xtr = cnt["v"]

                def f(gpsimd, s_xtr=s_xtr):
                    w(gpsimd, "v", s_xtr)
                    gpsimd.nop().then_inc(sems["xf"], 1)
                on("gpsimd", f)
                cnt["xf"] += 1
                if g + 2 < NCH:
                    emit_xload(g + 2)

            def f(vector):
                w(vector, "s")
                w(vector, "g")
                li = vector.tensor_sub(s_xx[:, :, 1:MAC], s_xf[:, :, 0:MAC - 1], s_xf[:, :, 1:MAC])
                li = vector.tensor_sub(s_xx[:, :, 0:1], s_pc[:], s_xf[:, :, 0:1])
                li.then_inc(sems["v"], 1)
                cnt["v"] += 1
                vector.wait_ge(sems["v"], cnt["v"])
                for s in range(SL):
                    li = vector.scalar_tensor_tensor(s_mx[:, s, :], s_xx[:, s, :], s_tv[:, s, 0:1],
                                                s_xf[:, s, :], ALU.mult, ALU.add)
                li = vector.tensor_copy(s_pc[:], s_xf[:, :, MAC - 1:MAC])
                li.then_inc(sems["v"], 1)
            on("vector", f)
            cnt["v"] += 1
            v_xxx = cnt["v"]

            for t_i, tgt in enumerate(TGT):
                if tgt not in tgts:
                    continue
                def f(tensor, t_i=t_i, v_xxx=v_xxx):
                    w(tensor, "v", v_xxx)
                    w(tensor, "s")
                    for dc in range(SL):
                        li = tensor.matmul(p_A[0:32, 0:MAC], s_w1[:, dc, 32 * t_i:32 * (t_i + 1)],
                                      s_mx[:, dc, :], start=(dc == 0), stop=(dc == SL - 1))
                    li.then_inc(sems["t"], 1)
                on("tensor", f)
                cnt["t"] += 1
                tval = cnt["t"]

                def f(scalar, t_i=t_i, tval=tval):
                    w(scalar, "t", tval)
                    li = scalar.activation(s_m5[0:32, t_i, :], p_A[0:32, 0:MAC], AF.Tanh)
                    li.then_inc(sems["s"], 1)
                on("scalar", f)
                cnt["s"] += 1
            s_m5d = cnt["s"]

            for t_i, tgt in enumerate(TGT):
                if tgt not in tgts:
                    continue
                for s in range(SL):
                    def f(tensor, t_i=t_i, s=s):
                        w(tensor, "s")
                        w(tensor, "g")
                        if s == 0:
                            w(tensor, "v")
                        pp = p_A if s % 2 == 0 else p_Z
                        li = tensor.matmul(pp[:, 0:MAC], s_w2[:, t_i, 128 * s:128 * (s + 1)],
                                      s_m5[:, t_i, :], start=True, stop=True)
                        li.then_inc(sems["t"], 1)
                    on("tensor", f)
                    cnt["t"] += 1
                    tval = cnt["t"]

                    def f(vector, s=s, tval=tval):
                        w(vector, "t", tval)
                        pp = p_A if s % 2 == 0 else p_Z
                        li = vector.tensor_mul(s_mx[:, s, :], pp[:, 0:MAC], s_xx[:, s, :])
                        li.then_inc(sems["v"], 1)
                    on("vector", f)
                    cnt["v"] += 1
                    vval = cnt["v"]

                    def f(gpsimd, s=s, vval=vval):
                        w(gpsimd, "v", vval)
                        li = gpsimd.tensor_add(s_mx[:, s, :], s_mx[:, s, :], s_xf[:, s, :])
                        li.then_inc(sems["g"], 1)
                    on("gpsimd", f)
                    cnt["g"] += 1
                g_mix = cnt["g"]

                if tgt == "w":
                    def f(tensor, g_mix=g_mix):
                        w(tensor, "g", g_mix)
                        w(tensor, "s")
                        for dc in range(SL):
                            li = tensor.matmul(p_A[0:64, 0:MAC], s_d1[:, dc, :], s_mx[:, dc, :],
                                          start=(dc == 0), stop=(dc == SL - 1))
                        li.then_inc(sems["t"], 1)
                    on("tensor", f)
                    cnt["t"] += 1
                    tval = cnt["t"]

                    def f(scalar, tval=tval):
                        w(scalar, "t", tval)
                        li = scalar.activation(s_m6[:], p_A[0:64, 0:MAC], AF.Tanh)
                        li.then_inc(sems["s"], 1)
                    on("scalar", f)
                    cnt["s"] += 1
                elif tgt in ("k", "r"):
                    dst = s_kf if tgt == "k" else s_rf
                    for o in range(SL):
                        def f(tensor, o=o, g_mix=g_mix, tgt=tgt):
                            w(tensor, "g", g_mix)
                            w(tensor, "v")
                            if tgt == "r" and o == 0:
                                w(tensor, "ws", ws_r)
                            wmat = s_wk if tgt == "k" else s_ws
                            pp = p_B if o % 2 == 0 else p_A
                            for dc in range(SL):
                                li = tensor.matmul(pp[:, 0:MAC], wmat[:, dc, 128 * o:128 * (o + 1)],
                                              s_mx[:, dc, :], start=(dc == 0), stop=(dc == SL - 1))
                            li.then_inc(sems["t"], 1)
                        on("tensor", f)
                        cnt["t"] += 1
                        tval = cnt["t"]

                        def f(vector, o=o, tval=tval, dst=dst):
                            w(vector, "t", tval)
                            li = vector.tensor_copy(dst[:, o, :], (p_B if o % 2 == 0 else p_A)[:, 0:MAC])
                            li.then_inc(sems["v"], 1)
                        on("vector", f)
                        cnt["v"] += 1
                else:
                    wm = s_wv if tgt == "v" else s_ws
                    dstt = s_vt if tgt == "v" else s_gg
                    if tgt == "g":
                        def f(gpsimd, g_mix=g_mix):
                            w(gpsimd, "t")
                            li = gpsimd.dma_start(out=s_ws[:], in_=d_wg.rearrange("s p c -> p s c")).then_inc(sems["ws"], 16)
                        on("gpsimd", f)
                        cnt["ws"] += 16
                    ws_tg = cnt["ws"]
                    for cc in range(CPM):
                        def f(tensor, cc=cc, g_mix=g_mix, wm=wm, tgt=tgt, ws_tg=ws_tg):
                            w(tensor, "g", g_mix)
                            if tgt == "g" and cc == 0:
                                w(tensor, "ws", ws_tg)
                            w(tensor, "v")
                            w(tensor, "s")
                            for dc in range(SL):
                                for hh in range(2):
                                    li = tensor.matmul(p_Y[:, hh * 512:(hh + 1) * 512],
                                                  s_mx[:, dc, cc * CH:(cc + 1) * CH],
                                                  wm[:, dc, hh * 512:(hh + 1) * 512],
                                                  start=(dc == 0), stop=(dc == SL - 1))
                            li.then_inc(sems["t"], 1)
                        on("tensor", f)
                        cnt["t"] += 1
                        tval = cnt["t"]
                        if tgt == "v":
                            def f(vector, cc=cc, tval=tval):
                                w(vector, "t", tval)
                                li = vector.tensor_copy(s_vt[:, cc, :], p_Y[:])
                                li.then_inc(sems["v"], 1)
                            on("vector", f)
                            cnt["v"] += 1
                        else:
                            def f(scalar, cc=cc, tval=tval):
                                w(scalar, "t", tval)
                                li = scalar.activation(s_gg[:, cc, :], p_Y[:], AF.Silu)
                                li.then_inc(sems["s"], 1)
                            on("scalar", f)
                            cnt["s"] += 1

            if own:
                def f(gpsimd):
                    w(gpsimd, "t")
                    li = gpsimd.dma_start(out=s_ws[:], in_=d_wo.rearrange("s p c -> p s c")).then_inc(sems["ws"], 16)
                on("gpsimd", f)
                cnt["ws"] += 16
            ws_o = cnt["ws"]

            dbg("xf_m%d" % m, s_xf[:])
            dbg("xx_m%d" % m, s_xx[:])
            dbg("rf_m%d" % m, s_rf[:])
            dbg("kf_m%d" % m, s_kf[:])
            dbg("vt_m%d" % m, s_vt[:])
            dbg("gg_m%d" % m, s_gg[:])
            dbg("m6_m%d" % m, s_m6[:])

            for cc in range(CPM):
                g = m * CPM + cc
                if own:
                    epar = cc % 2
                    ename = "xe%d" % epar

                    def f(gpsimd, g=g, epar=epar, ename=ename):
                        w(gpsimd, "v")
                        li = gpsimd.dma_start(out=s_xep[:, epar, :],
                                         in_=d_x[1 + g * CH:1 + (g + 1) * CH, :]).then_inc(sems[ename], 16)
                    on("gpsimd", f)
                    cnt[ename] += 16
                    xe_w = cnt[ename]

                for q in range(2):
                    def f(tensor, cc=cc, q=q):
                        w(tensor, "s")
                        w(tensor, "v")
                        pp = p_A if q == 0 else p_Z
                        for i in range(4):
                            s = q * 4 + i
                            li = tensor.matmul(pp[:, i * CH:(i + 1) * CH],
                                          s_d2[:, 128 * s:128 * (s + 1)],
                                          s_m6[:, cc * CH:(cc + 1) * CH],
                                          start=True, stop=True)
                        li.then_inc(sems["t"], 1)
                    on("tensor", f)
                    cnt["t"] += 1
                    tval = cnt["t"]

                    def f(vector, q=q, tval=tval):
                        w(vector, "t", tval)
                        pp = p_A if q == 0 else p_Z
                        li = vector.tensor_add(s_ew[:, q * 4:(q + 1) * 4, :],
                                          pp[:, 0:512].rearrange("p (s c) -> p s c", s=4),
                                          s_tv[:, q * 4:(q + 1) * 4, 1:2].broadcast_to([128, 4, CH]))
                        li.then_inc(sems["v"], 1)
                    on("vector", f)
                    cnt["v"] += 1
                    vval = cnt["v"]

                    def f(scalar, q=q, vval=vval):
                        w(scalar, "v", vval)
                        li = scalar.activation(s_ew[:, q * 4:(q + 1) * 4, :],
                                          s_ew[:, q * 4:(q + 1) * 4, :], AF.Exp)
                        li.then_inc(sems["s"], 1)
                    on("scalar", f)
                    cnt["s"] += 1
                s_ewd = cnt["s"]

                def f(vector, s_ewd=s_ewd, own=own):
                    w(vector, "s", s_ewd)
                    for s in range(SL):
                        li = vector.tensor_tensor_scan(s_L[:, s, 1:CH + 1], s_ew[:, s, :],
                                                  s_n1[:], 0.0, ALU.subtract, ALU.mult)
                    li.then_inc(sems["v"], 1)
                    cnt["v"] += 1
                    vector.wait_ge(sems["v"], cnt["v"])
                    li = vector.tensor_scalar_mul(s_ti[:, :, 0:1], s_L[:, :, CH:CH + 1], 0.5)
                    li = vector.tensor_copy(s_ti[:, :, 1:2], s_L[:, :, CH:CH + 1])
                    li.then_inc(sems["v"], 1)
                    cnt["v"] += 1
                    vector.wait_ge(sems["v"], cnt["v"])
                    if own:
                        li = vector.tensor_sub(s_L[:, :, 1:CH + 1], s_L[:, :, 1:CH + 1],
                                          s_ti[:, :, 0:1].broadcast_to([128, SL, CH]))
                        li = vector.tensor_scalar_mul(s_L[:, :, 0:1], s_ti[:, :, 0:1], -1.0)
                    else:
                        li = vector.tensor_sub(s_L[:, :, 1:CH + 1],
                                          s_ti[:, :, 1:2].broadcast_to([128, SL, CH]),
                                          s_L[:, :, 1:CH + 1])
                    li.then_inc(sems["v"], 1)
                on("vector", f)
                cnt["v"] += 1
                v_scan = cnt["v"]
                dbg("ew_c%d" % g, s_ew[:])

                def f(scalar, v_scan=v_scan):
                    w(scalar, "v", v_scan)
                    li = scalar.activation(s_te[:], s_ti[:], AF.Exp)
                    li.then_inc(sems["s"], 1)
                on("scalar", f)
                cnt["s"] += 1

                def f(scalar, own=own, v_scan=v_scan):
                    w(scalar, "v", v_scan)
                    if own:
                        li = scalar.activation(s_D1[:], s_L[:, :, 0:CH], AF.Exp)
                        li = scalar.activation(s_D2[:], s_L[:, :, 1:CH + 1], AF.Exp, scale=-1.0)
                    else:
                        li = scalar.activation(s_D2[:], s_L[:, :, 1:CH + 1], AF.Exp)
                    li.then_inc(sems["s"], 1)
                on("scalar", f)
                cnt["s"] += 1
                s_dd = cnt["s"]
                dbg("D2_c%d" % g, s_D2[:])
                dbg("te_c%d" % g, s_te[:])
                dbg("ti_c%d" % g, s_ti[:])
                dbg("La_c%d" % g, s_L[:])

                def f(vector, cc=cc, own=own, s_dd=s_dd):
                    w(vector, "s", s_dd)
                    w(vector, "t")
                    sl_tok = slice(cc * CH, (cc + 1) * CH)
                    li = vector.tensor_mul(s_Kp[:], s_kf[:, :, sl_tok], s_D2[:])
                    if own:
                        li = vector.tensor_mul(s_Rp[:], s_rf[:, :, sl_tok], s_D1[:])
                        for s in range(SL):
                            li = vector.scalar_tensor_tensor(s_P2[:, s, :], s_rf[:, s, sl_tok],
                                                        s_tv[:, s, 2:3], s_kf[:, s, sl_tok],
                                                        ALU.mult, ALU.mult)
                        li = vector.tensor_mul(s_Sc[:], s_S[:],
                                          s_te[:, :, 0:1].broadcast_to([128, SL, HK]))
                    li.then_inc(sems["v"], 1)
                on("vector", f)
                cnt["v"] += 1
                v_rk = cnt["v"]

                for half in range(2):
                    def f(tensor, half=half, v_rk=v_rk):
                        w(tensor, "v")
                        for i in range(4):
                            s = half * 4 + i
                            li = tensor.transpose(p_T[:, i * 128:(i + 1) * 128], s_Kp[:, s, :], s_id[:])
                        li.then_inc(sems["t"], 1)
                    on("tensor", f)
                    cnt["t"] += 1
                    tval = cnt["t"]

                    def f(vector, half=half, tval=tval):
                        w(vector, "t", tval)
                        li = vector.tensor_copy(s_Kt[:, half * 4:(half + 1) * 4, :],
                                    p_T[:].rearrange("p (s c) -> p s c", s=4))
                        li.then_inc(sems["v"], 1)
                    on("vector", f)
                    cnt["v"] += 1
                s_ktd = cnt["v"]

                t_yall = None
                if own:
                    for grp in range(4):
                        heads = [(grp % 2) + 8 * (grp // 2) + 2 * i for i in range(4)]
                        p_M = p_A if grp % 2 == 0 else p_B

                        def f(tensor, grp=grp, heads=heads, p_M=p_M):
                            w(tensor, "v")
                            for i, h in enumerate(heads):
                                s, pb = h // 2, (h % 2) * 64
                                li = tensor.matmul(p_M[:, i * CH:(i + 1) * CH],
                                              s_Kp[pb:pb + 64, s, :], s_Rp[pb:pb + 64, s, :],
                                              start=True, stop=True)
                            li.then_inc(sems["t"], 1)
                        on("tensor", f)
                        cnt["t"] += 1
                        tval = cnt["t"]

                        def f(vector, tval=tval, p_M=p_M):
                            w(vector, "t", tval)
                            li = vector.tensor_mul(s_M[:], p_M[:].rearrange("p (q c) -> p q c", q=4),
                                              s_mkk[:].rearrange("p (q c) -> p q c", q=1).broadcast_to([128, 4, CH]))
                            li.then_inc(sems["v"], 1)
                        on("vector", f)
                        cnt["v"] += 1
                        vval = cnt["v"]

                        def f(tensor, grp=grp, cc=cc, vval=vval, s_ktd=s_ktd, heads=heads):
                            w(tensor, "v", vval)
                            w(tensor, "v", s_ktd)
                            for i, h in enumerate(heads):
                                s, pb = h // 2, (h % 2) * 64
                                co = h * 64
                                li = tensor.matmul(p_Y[:, co:co + 64], s_M[:, i, :],
                                              s_vt[:, cc, co:co + 64], start=True, stop=False)
                                li = tensor.matmul(p_Y[:, co:co + 64], s_Rp[pb:pb + 64, s, :],
                                              s_Sc[pb:pb + 64, s, :], start=False, stop=True)
                            li.then_inc(sems["t"], 1)
                        on("tensor", f)
                        cnt["t"] += 1
                    t_yall = cnt["t"]

                def f(tensor, cc=cc, s_ktd=s_ktd):
                    w(tensor, "v")
                    for s in range(SL):
                        for hh in range(2):
                            li = tensor.matmul(p_B[hh * 64:hh * 64 + 64, s * 64:(s + 1) * 64],
                                          s_Kt[:, s, hh * 64:(hh + 1) * 64],
                                          s_vt[:, cc, (2 * s + hh) * 64:(2 * s + hh + 1) * 64],
                                          start=True, stop=True)
                    li.then_inc(sems["t"], 1)
                on("tensor", f)
                cnt["t"] += 1
                t_state = cnt["t"]

                def f(vector, own=own, t_state=t_state):
                    w(vector, "t", t_state)
                    e1a = s_ep1[:, 0:512].rearrange("p (s j) -> p s j", s=SL)
                    e1b = s_ep1[:, 512:1024].rearrange("p (s j) -> p s j", s=SL)
                    li = vector.tensor_mul(e1a, s_S[:], s_te[:, :, 1:2].broadcast_to([128, SL, HK]))
                    if own:
                        li = vector.tensor_mul(e1b, p_B[:].rearrange("p (s j) -> p s j", s=SL),
                                          s_te[:, :, 0:1].broadcast_to([128, SL, HK]))
                        li.then_inc(sems["v"], 1)
                        cnt["v"] += 1
                        vector.wait_ge(sems["v"], cnt["v"])
                        li = vector.tensor_add(s_S[:], e1a, e1b)
                    else:
                        li = vector.tensor_add(s_S[:], e1a, p_B[:].rearrange("p (s j) -> p s j", s=SL))
                    li.then_inc(sems["v"], 1)
                on("vector", f)
                cnt["v"] += 1

                dbg("L_c%d" % g, s_L[:])
                dbg("S_c%d" % g, s_S[:])

                if not own:
                    continue

                for half in range(2):
                    def f(tensor, half=half, t_yall=t_yall):
                        w(tensor, "s")
                        w(tensor, "v")
                        for i in range(4):
                            s = half * 4 + i
                            li = tensor.transpose(p_T[:, i * 128:(i + 1) * 128], s_P2[:, s, :], s_id[:])
                        li.then_inc(sems["t"], 1)
                    on("tensor", f)
                    cnt["t"] += 1
                    tval = cnt["t"]

                    def f(vector, half=half, tval=tval):
                        w(vector, "t", tval)
                        li = vector.tensor_copy(s_Kt[:, half * 4:(half + 1) * 4, :],
                                    p_T[:].rearrange("p (s c) -> p s c", s=4))
                        li.then_inc(sems["v"], 1)
                    on("vector", f)
                    cnt["v"] += 1
                s_p2t = cnt["v"]

                def f(vector, cc=cc, t_yall=t_yall, s_p2t=s_p2t):
                    w(vector, "t", t_yall)
                    li = vector.tensor_reduce(s_dg[:].rearrange("p (h o) -> p h o", o=1),
                                         s_Kt[:].rearrange("p s c -> p (s c)").rearrange("p (h j) -> p h j", h=H),
                                         AXX, ALU.add)
                    li.then_inc(sems["v"], 1)
                    cnt["v"] += 1
                    vector.wait_ge(sems["v"], cnt["v"])
                    li = vector.tensor_mul(s_zg[:].rearrange("p (h j) -> p h j", h=H),
                                      s_vt[:, cc, :].rearrange("p (h j) -> p h j", h=H),
                                      s_dg[:].rearrange("p (h o) -> p h o", o=1).broadcast_to([128, H, HK]))
                    li.then_inc(sems["v"], 1)
                    cnt["v"] += 1
                    vector.wait_ge(sems["v"], cnt["v"])
                    li = vector.tensor_add(s_ep2[:], p_Y[:], s_zg[:])
                    li.then_inc(sems["v"], 1)
                on("vector", f)
                cnt["v"] += 1
                if ("y_c%d" % g) in dbg_out:
                    dbg("y_c%d" % g, s_ep2[:])

                def f(vector):
                    yv = s_ep2[:].rearrange("p (h j) -> p h j", h=H)
                    li = vector.tensor_reduce(s_st1[:].rearrange("p (h o) -> p h o", o=1), yv, AXX, ALU.add)
                    li.then_inc(sems["v"], 1)
                    cnt["v"] += 1
                    vector.wait_ge(sems["v"], cnt["v"])
                    li = vector.tensor_scalar_mul(s_st1[:], s_st1[:], 1.0 / HK)
                    li.then_inc(sems["v"], 1)
                    cnt["v"] += 1
                    vector.wait_ge(sems["v"], cnt["v"])
                    li = vector.tensor_sub(s_ep1[:].rearrange("p (h j) -> p h j", h=H), yv,
                                      s_st1[:].rearrange("p (h o) -> p h o", o=1).broadcast_to([128, H, HK]))
                    li.then_inc(sems["v"], 1)
                on("vector", f)
                cnt["v"] += 1
                vval = cnt["v"]
                dbg("yc_c%d" % g, s_ep1[:])

                def f(vector, vval=vval):
                    li = vector.tensor_mul(s_zg[:], s_ep1[:], s_ep1[:])
                    li.then_inc(sems["v"], 1)
                    cnt["v"] += 1
                    vector.wait_ge(sems["v"], cnt["v"])
                    li = vector.tensor_reduce(s_st2[:].rearrange("p (h o) -> p h o", o=1),
                                         s_zg[:].rearrange("p (h j) -> p h j", h=H), AXX, ALU.add)
                    li.then_inc(sems["v"], 1)
                    cnt["v"] += 1
                    vector.wait_ge(sems["v"], cnt["v"])
                    li = vector.tensor_scalar(s_st2[:], s_st2[:], 1.0 / HK, EPS_GN, ALU.mult, ALU.add)
                    li.then_inc(sems["v"], 1)
                on("vector", f)
                cnt["v"] += 1
                vval = cnt["v"]
                dbg("st1_c%d" % g, s_st1[:])
                dbg("var_c%d" % g, s_st2[:])

                def f(scalar, vval=vval):
                    w(scalar, "v", vval)
                    li = scalar.activation(s_st2[:], s_st2[:], AF.Ln)
                    li.then_inc(sems["s"], 1)
                    cnt["s"] += 1
                    scalar.wait_ge(sems["s"], cnt["s"])
                    li = scalar.activation(s_st2[:], s_st2[:], AF.Exp, scale=-0.5)
                    li.then_inc(sems["s"], 1)
                on("scalar", f)
                cnt["s"] += 1
                sval = cnt["s"]
                dbg("rstd_c%d" % g, s_st2[:])

                def f(vector, cc=cc, sval=sval):
                    w(vector, "s", sval)
                    li = vector.tensor_mul(s_ep1[:].rearrange("p (h j) -> p h j", h=H),
                                      s_ep1[:].rearrange("p (h j) -> p h j", h=H),
                                      s_st2[:].rearrange("p (h o) -> p h o", o=1).broadcast_to([128, H, HK]))
                    li = vector.tensor_mul(s_ep1[:], s_ep1[:], s_gb[:, 0, :])
                    li = vector.tensor_add(s_ep1[:], s_ep1[:], s_gb[:, 1, :])
                    li = vector.tensor_mul(s_zg[:], s_ep1[:], s_gg[:, cc, :])
                    li.then_inc(sems["v"], 1)
                on("vector", f)
                cnt["v"] += 1
                v_zg = cnt["v"]
                dbg("zg1_c%d" % g, s_zg[:])

                for half in range(2):
                    def f(tensor, half=half, v_zg=v_zg):
                        w(tensor, "v")
                        for i in range(4):
                            s = half * 4 + i
                            li = tensor.transpose(p_T[:, i * 128:(i + 1) * 128],
                                             s_zg[:, 128 * s:128 * (s + 1)], s_id[:])
                        li.then_inc(sems["t"], 1)
                    on("tensor", f)
                    cnt["t"] += 1
                    tval = cnt["t"]

                    def f(vector, half=half, tval=tval):
                        w(vector, "t", tval)
                        li = vector.tensor_copy(s_zf[:, half * 4:(half + 1) * 4, :],
                                    p_T[:].rearrange("p (s c) -> p s c", s=4))
                        li.then_inc(sems["v"], 1)
                    on("vector", f)
                    cnt["v"] += 1
                s_zfd = cnt["v"]

                def f(tensor, s_zfd=s_zfd, ws_o=ws_o):
                    w(tensor, "v", s_zfd)
                    w(tensor, "ws", ws_o)
                    w(tensor, "v")
                    for dc in range(SL):
                        for hh in range(2):
                            li = tensor.matmul(p_Z[:, hh * 512:(hh + 1) * 512], s_zf[:, dc, :],
                                          s_ws[:, dc, hh * 512:(hh + 1) * 512],
                                          start=(dc == 0), stop=(dc == SL - 1))
                    li.then_inc(sems["t"], 1)
                on("tensor", f)
                cnt["t"] += 1
                t_att = cnt["t"]

                def f(vector, cc=cc, t_att=t_att, xe_w=xe_w, ename=ename, epar=epar):
                    w(vector, "t", t_att)
                    w(vector, ename, xe_w)
                    li = vector.tensor_add(s_ep1[:], p_Z[:], s_xep[:, epar, :])
                    li.then_inc(sems["v"], 1)
                on("vector", f)
                cnt["v"] += 1
                vval = cnt["v"]
                dbg("res_c%d" % g, s_ep1[:])

                def f(vector, vval=vval):
                    li = vector.tensor_reduce(s_st3[:, 0:1], s_ep1[:], AXX, ALU.add)
                    li.then_inc(sems["v"], 1)
                    cnt["v"] += 1
                    vector.wait_ge(sems["v"], cnt["v"])
                    li = vector.tensor_scalar_mul(s_st3[:, 0:1], s_st3[:, 0:1], 1.0 / C)
                    li.then_inc(sems["v"], 1)
                    cnt["v"] += 1
                    vector.wait_ge(sems["v"], cnt["v"])
                    li = vector.tensor_scalar_sub(s_ep1[:], s_ep1[:], s_st3[:, 0:1])
                    li.then_inc(sems["v"], 1)
                    cnt["v"] += 1
                    vector.wait_ge(sems["v"], cnt["v"])
                    li = vector.tensor_mul(s_ep2[:], s_ep1[:], s_ep1[:])
                    li.then_inc(sems["v"], 1)
                    cnt["v"] += 1
                    vector.wait_ge(sems["v"], cnt["v"])
                    li = vector.tensor_reduce(s_st3[:, 1:2], s_ep2[:], AXX, ALU.add)
                    li.then_inc(sems["v"], 1)
                    cnt["v"] += 1
                    vector.wait_ge(sems["v"], cnt["v"])
                    li = vector.tensor_scalar(s_st3[:, 1:2], s_st3[:, 1:2], 1.0 / C, EPS_LN, ALU.mult, ALU.add)
                    li.then_inc(sems["v"], 1)
                on("vector", f)
                cnt["v"] += 1
                vval = cnt["v"]

                def f(scalar, vval=vval):
                    w(scalar, "v", vval)
                    li = scalar.activation(s_st3[:, 1:2], s_st3[:, 1:2], AF.Ln)
                    li.then_inc(sems["s"], 1)
                    cnt["s"] += 1
                    scalar.wait_ge(sems["s"], cnt["s"])
                    li = scalar.activation(s_st3[:, 1:2], s_st3[:, 1:2], AF.Exp, scale=-0.5)
                    li.then_inc(sems["s"], 1)
                on("scalar", f)
                cnt["s"] += 1
                sval = cnt["s"]

                def f(vector, sval=sval):
                    w(vector, "s", sval)
                    li = vector.tensor_scalar_mul(s_zg[:], s_ep1[:], s_st3[:, 1:2])
                    li.then_inc(sems["v"], 1)
                on("vector", f)
                cnt["v"] += 1
                v_x1 = cnt["v"]

                tok0 = (m - n_mac_pre) * MAC + cc * CH
                for half in range(2):
                    def f(tensor, half=half, v_x1=v_x1):
                        w(tensor, "v")
                        for i in range(4):
                            s = half * 4 + i
                            li = tensor.transpose(p_T[:, i * 128:(i + 1) * 128],
                                             s_zg[:, 128 * s:128 * (s + 1)], s_id[:])
                        li.then_inc(sems["t"], 1)
                    on("tensor", f)
                    cnt["t"] += 1
                    tval = cnt["t"]

                    def f(vector, half=half, tval=tval, tok0=tok0):
                        w(vector, "t", tval)
                        li = vector.tensor_copy(s_x1[:, half * 4:(half + 1) * 4, tok0:tok0 + CH],
                                    p_T[:].rearrange("p (s c) -> p s c", s=4))
                        li.then_inc(sems["v"], 1)
                    on("vector", f)
                    cnt["v"] += 1

                dbg("x1_c%d" % g, s_x1[:, :, tok0:tok0 + CH])

        dbg("x1", s_x1[:])

        inner.close()

        if do_ffn:
            NT128 = (n_mac_own * MAC) // 128
            NT512 = (n_mac_own * MAC) // 512
            s_acc = E(nc.sbuf_tensor("s_acc", [128, NT128, C], F32))
            s_wu = E(nc.sbuf_tensor("s_wu", [128, 2, SL, 1024], BF16))
            s_wd2 = E(nc.sbuf_tensor("s_wd2", [128, 2, SL, C], BF16))
            s_h = E(nc.sbuf_tensor("s_h", [128, SL, 512], BF16))
            s_cmb = E(nc.sbuf_tensor("s_cmb", [128, NT128, 8], F32))
            s_lg = E(nc.sbuf_tensor("s_lg", [128, NT128, 8], F32))
            s_q = E(nc.sbuf_tensor("s_q", [128, NT128, 8], F32))
            s_r8 = E(nc.sbuf_tensor("s_r8", [128, NT128, 4], F32))
            s_lgf = E(nc.sbuf_tensor("s_lgf", [8, 512], F32))
            s_rw = E(nc.sbuf_tensor("s_rw", [128, SL, 8], BF16))
            s_rb = E(nc.sbuf_tensor("s_rb", [8, 1], F32))
            s_if = E(nc.sbuf_tensor("s_if", [128, 128], F32))
            s_out = E(nc.sbuf_tensor("s_out", [128, 2, C], F32))

            bar = {k: cnt[k] for k in ["t", "v", "s", "g"]}
            for eng in ["tensor", "vector", "scalar", "gpsimd", "sync"]:
                def f(e, bar=bar):
                    for k, vv in bar.items():
                        w(e, k, vv)
                on(eng, f)

            def f(sync):
                li = sync.dma_start(out=s_rb[:], in_=d_rb).then_inc(sems["c0"], 16)
                li = sync.dma_start(out=s_if[:], in_=i_idf.ap()).then_inc(sems["c0"], 16)
            on("sync", f)
            cnt["c0"] += 32

            def f(gpsimd):
                li = gpsimd.dma_start(out=s_rw[:], in_=d_rw.rearrange("s p c -> p s c")).then_inc(sems["wd"], 16)
            on("gpsimd", f)
            cnt["wd"] += 16
            x0_ffn = cnt["c0"]
            wd_ffn = cnt["wd"]

            def f(vector):
                w(vector, "c0", x0_ffn)
                w(vector, "wd", wd_ffn)
                li = vector.memset(s_acc[:], 0.0)
                li.then_inc(sems["v"], 1)
            on("vector", f)
            cnt["v"] += 1

            for tt in range(NT512):
                def f(tensor, tt=tt):
                    w(tensor, "v")
                    for dc in range(SL):
                        li = tensor.matmul(p_A[0:8, 0:512], s_rw[:, dc, :],
                                      s_x1[:, dc, tt * 512:(tt + 1) * 512],
                                      start=(dc == 0), stop=(dc == SL - 1))
                    li.then_inc(sems["t"], 1)
                on("tensor", f)
                cnt["t"] += 1
                tval = cnt["t"]

                def f(vector, tval=tval):
                    w(vector, "t", tval)
                    li = vector.tensor_scalar_add(s_lgf[:], p_A[0:8, 0:512], s_rb[:])
                    li.then_inc(sems["v"], 1)
                on("vector", f)
                cnt["v"] += 1
                vval = cnt["v"]

                def f(tensor, vval=vval):
                    w(tensor, "v", vval)
                    for i in range(4):
                        li = tensor.transpose(p_B[:, i * 8:(i + 1) * 8],
                                         s_lgf[:, i * 128:(i + 1) * 128], s_if[0:8, 0:8])
                    li.then_inc(sems["t"], 1)
                on("tensor", f)
                cnt["t"] += 1
                tval = cnt["t"]

                def f(vector, tt=tt, tval=tval):
                    w(vector, "t", tval)
                    li = vector.tensor_copy(s_lg[:, tt * 4:(tt + 1) * 4, :],
                                       p_B[:, 0:32].rearrange("p (c e) -> p c e", e=8))
                    li.then_inc(sems["v"], 1)
                on("vector", f)
                cnt["v"] += 1

            def f(vector):
                li = vector.tensor_reduce(s_r8[:, :, 0:1], s_lg[:], AXX, ALU.max)
                li.then_inc(sems["v"], 1)
                cnt["v"] += 1
                vector.wait_ge(sems["v"], cnt["v"])
                li = vector.tensor_sub(s_q[:], s_lg[:], s_r8[:, :, 0:1].broadcast_to([128, NT128, 8]))
                li.then_inc(sems["v"], 1)
            on("vector", f)
            cnt["v"] += 1
            vval = cnt["v"]

            def f(scalar, vval=vval):
                w(scalar, "v", vval)
                li = scalar.activation(s_q[:], s_q[:], AF.Exp)
                li.then_inc(sems["s"], 1)
            on("scalar", f)
            cnt["s"] += 1
            sval = cnt["s"]

            def f(vector, sval=sval):
                w(vector, "s", sval)
                li = vector.tensor_single_scalar(s_cmb[:], s_q[:], 1.0, ALU.is_lt)
                li.then_inc(sems["v"], 1)
                cnt["v"] += 1
                vector.wait_ge(sems["v"], cnt["v"])
                li = vector.tensor_mul(s_cmb[:], s_cmb[:], s_q[:])
                li.then_inc(sems["v"], 1)
                cnt["v"] += 1
                vector.wait_ge(sems["v"], cnt["v"])
                li = vector.tensor_reduce(s_r8[:, :, 1:2], s_cmb[:], AXX, ALU.max)
                li.then_inc(sems["v"], 1)
                cnt["v"] += 1
                vector.wait_ge(sems["v"], cnt["v"])
                li = vector.tensor_scalar_add(s_r8[:, :, 2:3], s_r8[:, :, 1:2], 1.0)
                li.then_inc(sems["v"], 1)
                cnt["v"] += 1
                vector.wait_ge(sems["v"], cnt["v"])
                li = vector.reciprocal(s_r8[:, :, 3:4], s_r8[:, :, 2:3])
                li.then_inc(sems["v"], 1)
                cnt["v"] += 1
                vector.wait_ge(sems["v"], cnt["v"])
                li = vector.tensor_tensor(s_lg[:], s_q[:], s_r8[:, :, 1:2].broadcast_to([128, NT128, 8]), ALU.is_ge)
                li.then_inc(sems["v"], 1)
                cnt["v"] += 1
                vector.wait_ge(sems["v"], cnt["v"])
                li = vector.tensor_mul(s_q[:], s_q[:], s_lg[:])
                li.then_inc(sems["v"], 1)
                cnt["v"] += 1
                vector.wait_ge(sems["v"], cnt["v"])
                li = vector.tensor_mul(s_cmb[:], s_q[:], s_r8[:, :, 3:4].broadcast_to([128, NT128, 8]))
                li.then_inc(sems["v"], 1)
            on("vector", f)
            cnt["v"] += 1
            dbg("combine", s_cmb[:])

            tdone = []
            for eidx in range(NE):
                for fh in range(2):
                    idx = eidx * 2 + fh
                    ring = idx % 2
                    wsem = "w%d" % ring

                    def f(gpsimd, eidx=eidx, fh=fh, ring=ring, wsem=wsem, idx=idx):
                        if idx >= 2:
                            w(gpsimd, "t", tdone[idx - 2])
                        li = gpsimd.dma_start(out=s_wu[:, ring, :, :],
                                         in_=d_up[eidx, :, :, fh * 1024:(fh + 1) * 1024].rearrange("s p f -> p s f")
                                         ).then_inc(sems[wsem], 16)
                        li = gpsimd.dma_start(out=s_wd2[:, ring, :, :],
                                         in_=d_dn[eidx, fh * 8:(fh + 1) * 8, :, :].rearrange("s p f -> p s f")
                                         ).then_inc(sems[wsem], 16)
                    on("gpsimd", f)
                    cnt[wsem] += 32
                    wv = cnt[wsem]

                    for tt in range(NT512):
                        for fp in range(4):
                            def f(tensor, ring=ring, tt=tt, fp=fp, wsem=wsem, wv=wv):
                                if fp == 0 and tt == 0:
                                    w(tensor, wsem, wv)
                                w(tensor, "s")
                                for pp, fc in ((p_A, 2 * fp), (p_B, 2 * fp + 1)):
                                    for dc in range(SL):
                                        li = tensor.matmul(pp[:, 0:512], s_wu[:, ring, dc, fc * 128:(fc + 1) * 128],
                                                      s_x1[:, dc, tt * 512:(tt + 1) * 512],
                                                      start=(dc == 0), stop=(dc == SL - 1))
                                li.then_inc(sems["t"], 1)
                            on("tensor", f)
                            cnt["t"] += 1
                            tval = cnt["t"]

                            def f(scalar, fp=fp, tval=tval):
                                w(scalar, "t", tval)
                                li = scalar.activation(s_h[:, 2 * fp, :], p_A[:, 0:512], AF.Silu)
                                li = scalar.activation(s_h[:, 2 * fp + 1, :], p_B[:, 0:512], AF.Silu)
                                li.then_inc(sems["s"], 1)
                            on("scalar", f)
                            cnt["s"] += 1
                        s_hd = cnt["s"]

                        for st in range(4):
                            t128 = tt * 4 + st
                            pd = p_Y if st % 2 == 0 else p_Z

                            def f(tensor, ring=ring, st=st, s_hd=s_hd, pd=pd):
                                w(tensor, "s", s_hd)
                                w(tensor, "v")
                                for fc in range(SL):
                                    for hh in range(2):
                                        li = tensor.matmul(pd[:, hh * 512:(hh + 1) * 512],
                                                      s_h[:, fc, st * 128:(st + 1) * 128],
                                                      s_wd2[:, ring, fc, hh * 512:(hh + 1) * 512],
                                                      start=(fc == 0), stop=(fc == SL - 1))
                                li.then_inc(sems["t"], 1)
                            on("tensor", f)
                            cnt["t"] += 1
                            tval = cnt["t"]

                            def f(vector, eidx=eidx, t128=t128, tval=tval, pd=pd):
                                w(vector, "t", tval)
                                if eidx < 8:
                                    li = vector.scalar_tensor_tensor(s_acc[:, t128, :], pd[:],
                                                                s_cmb[:, t128, eidx:eidx + 1],
                                                                s_acc[:, t128, :], ALU.mult, ALU.add)
                                else:
                                    li = vector.tensor_add(s_acc[:, t128, :], pd[:], s_acc[:, t128, :])
                                li.then_inc(sems["v"], 1)
                            on("vector", f)
                            cnt["v"] += 1
                    tdone.append(cnt["t"])

            dbg("ffnacc", s_acc[:, 0, :])
            for t128 in range(NT128):
                for half in range(2):
                    def f(tensor, t128=t128, half=half):
                        w(tensor, "v")
                        for i in range(4):
                            s = half * 4 + i
                            li = tensor.transpose(p_T[:, i * 128:(i + 1) * 128],
                                             s_x1[:, s, t128 * 128:(t128 + 1) * 128], s_id[:])
                        li.then_inc(sems["t"], 1)
                    on("tensor", f)
                    cnt["t"] += 1
                    tval = cnt["t"]

                    def f(vector, half=half, tval=tval):
                        w(vector, "t", tval)
                        li = vector.tensor_copy(s_zg[:, half * 512:(half + 1) * 512], p_T[:])
                        li.then_inc(sems["v"], 1)
                    on("vector", f)
                    cnt["v"] += 1

                if t128 == 0:
                    dbg("x1tm0", s_zg[:])

                if t128 == 0:
                    dbg("fzg0", s_zg[:])

                def f(vector, t128=t128):
                    w(vector, "s")
                    li = vector.tensor_add(s_ep1[:], s_acc[:, t128, :], s_zg[:])
                    li.then_inc(sems["v"], 1)
                on("vector", f)
                cnt["v"] += 1
                vval = cnt["v"]
                if t128 == 0:
                    dbg("fres0", s_ep1[:])
                    dbg("facc0", s_acc[:, 0, :])

                def f(vector, vval=vval):
                    w(vector, "st")
                    li = vector.tensor_reduce(s_st3[:, 0:1], s_ep1[:], AXX, ALU.add)
                    li.then_inc(sems["v"], 1)
                    cnt["v"] += 1
                    vector.wait_ge(sems["v"], cnt["v"])
                    li = vector.tensor_scalar_mul(s_st3[:, 0:1], s_st3[:, 0:1], 1.0 / C)
                    li.then_inc(sems["v"], 1)
                    cnt["v"] += 1
                    vector.wait_ge(sems["v"], cnt["v"])
                    li = vector.tensor_scalar_sub(s_ep1[:], s_ep1[:], s_st3[:, 0:1])
                    li.then_inc(sems["v"], 1)
                    cnt["v"] += 1
                    vector.wait_ge(sems["v"], cnt["v"])
                    li = vector.tensor_mul(s_ep2[:], s_ep1[:], s_ep1[:])
                    li.then_inc(sems["v"], 1)
                    cnt["v"] += 1
                    vector.wait_ge(sems["v"], cnt["v"])
                    li = vector.tensor_reduce(s_st3[:, 1:2], s_ep2[:], AXX, ALU.add)
                    li.then_inc(sems["v"], 1)
                    cnt["v"] += 1
                    vector.wait_ge(sems["v"], cnt["v"])
                    li = vector.tensor_scalar(s_st3[:, 1:2], s_st3[:, 1:2], 1.0 / C, EPS_LN, ALU.mult, ALU.add)
                    li.then_inc(sems["v"], 1)
                on("vector", f)
                cnt["v"] += 1
                vval = cnt["v"]

                def f(scalar, vval=vval):
                    w(scalar, "v", vval)
                    li = scalar.activation(s_st3[:, 1:2], s_st3[:, 1:2], AF.Ln)
                    li.then_inc(sems["s"], 1)
                    cnt["s"] += 1
                    scalar.wait_ge(sems["s"], cnt["s"])
                    li = scalar.activation(s_st3[:, 1:2], s_st3[:, 1:2], AF.Exp, scale=-0.5)
                    li.then_inc(sems["s"], 1)
                on("scalar", f)
                cnt["s"] += 1
                sval = cnt["s"]

                def f(vector, t128=t128, sval=sval):
                    w(vector, "s", sval)
                    li = vector.tensor_scalar_mul(s_out[:, t128 % 2, :], s_ep1[:], s_st3[:, 1:2])
                    li.then_inc(sems["v"], 1)
                on("vector", f)
                cnt["v"] += 1
                vval = cnt["v"]
                if t128 == 0:
                    dbg("fout0", s_out[:, 0, :])
                    dbg("fst3", s_st3[:])

                def f(sync, t128=t128, vval=vval):
                    w(sync, "v", vval)
                    li = sync.dma_start(out=d_out[t128 * 128:(t128 + 1) * 128, :],
                                   in_=s_out[:, t128 % 2, :]).then_inc(sems["st"], 16)
                on("sync", f)
                cnt["st"] += 16

        def f(sync):
            w(sync, "st")
        on("sync", f)

        def f(gpsimd):
            w(gpsimd, "db")
        on("gpsimd", f)

    return nc



def _prep_shared(inputs):
    A = lambda k: np.asarray(inputs[k], dtype=np.float32)
    shared = {}
    for nm, key in [("W_r", "L0_attn_W_r"), ("W_k", "L0_attn_W_k"), ("W_v", "L0_attn_W_v"),
                    ("W_g", "L0_attn_W_g"), ("W_o", "L0_attn_W_o")]:
        shared[nm] = _mk(A(key).reshape(SL, 128, C))
    shared["maa_w1"] = _mk(A("L0_attn_time_maa_w1").reshape(SL, 128, 160))
    tm = [A("L0_attn_time_maa_w"), A("L0_attn_time_maa_k"), A("L0_attn_time_maa_v"),
          A("L0_attn_time_maa_r"), A("L0_attn_time_maa_g")]
    w2 = A("L0_attn_time_maa_w2")
    shared["w2aug"] = _mk(np.concatenate([w2, np.stack(tm)[:, None, :]], axis=1).transpose(1, 0, 2))
    shared["dec_w1"] = _mk(A("L0_attn_time_decay_w1").reshape(SL, 128, 64))
    shared["dec_w2"] = _mk(A("L0_attn_time_decay_w2"))
    tvec = np.stack([A("L0_attn_time_maa_x"), A("L0_attn_time_decay"),
                     A("L0_attn_time_faaaa").reshape(-1)], axis=-1)
    shared["tvec"] = _mk(tvec.reshape(SL, 128, 3).transpose(1, 0, 2))
    gb = np.stack([A("L0_attn_ln_gamma"), A("L0_attn_ln_beta")])
    shared["lngb"] = _mk(np.broadcast_to(gb[None], (128, 2, C)))
    shared["router_wT"] = _mk(A("L0_ffn_router_w").T.reshape(SL, 128, 8))
    shared["router_b"] = _mk(A("L0_ffn_router_b").reshape(8, 1))
    up = np.concatenate([A("L0_ffn_w_up"), A("L0_ffn_shared_w_up")], axis=0)
    dn = np.concatenate([A("L0_ffn_w_down"), A("L0_ffn_shared_w_down")], axis=0)
    shared["wupT"] = _mk(up.transpose(0, 2, 1).reshape(NE, SL, 128, F_FF))
    shared["wdnT"] = _mk(dn.transpose(0, 2, 1).reshape(NE, 16, 128, C))
    return shared


def _core_x(xfull, core, t_win):
    s, p = core // 4, core % 4
    t0 = p * T_OWN
    lo = t0 + T_OWN - (t_win + 1)
    seq = xfull[s]
    if lo >= 0:
        return _mk(seq[lo:t0 + T_OWN].astype(np.float32))
    pad = np.zeros((-lo, C), np.float32)
    return _mk(np.concatenate([pad, seq[0:t0 + T_OWN]]).astype(np.float32))


def kernel(**inputs):
    shared = _prep_shared(inputs)
    xfull = np.asarray(inputs["L0_attn_x"], dtype=np.float32)
    n_pre, n_own = _N_PRE, _N_OWN
    t_win = (n_pre + n_own) * MAC
    nc = build_program(n_mac_pre=n_pre, n_mac_own=n_own, do_ffn=_DO_FFN, debug=_DEBUG)
    in_maps = []
    for core in range(N_CORE):
        mm = dict(shared)
        mm["xin"] = _core_x(xfull, core, t_win)
        in_maps.append(mm)
    res = run_bass_kernel_spmd(nc, in_maps, core_ids=list(range(N_CORE)))
    _LAST_RES[0] = res
    out = np.zeros((2, 8192, C), np.float32)
    for core in range(N_CORE):
        s, p = core // 4, core % 4
        out[s, p * T_OWN:(p + 1) * T_OWN] = np.asarray(res.results[core]["out"], dtype=np.float32)
    return out


# version 03 (final)
# speedup vs baseline: 1.2954x; 1.1122x over previous
import contextlib
import numpy as np
import ml_dtypes
import concourse.bass as bass
import concourse.mybir as mybir
from concourse.bass_utils import run_bass_kernel_spmd

F32 = mybir.dt.float32
BF16 = mybir.dt.bfloat16
AF = mybir.ActivationFunctionType
ALU = mybir.AluOpType
AXX = mybir.AxisListType.X
BF = ml_dtypes.bfloat16

C = 1024
SL = 8
H = 16
HK = 64
CH = 128
MAC = 512
CPM = MAC // CH
T_OWN = 2048
N_CORE = 8
EPS_GN = 1e-5 * 64.0
EPS_LN = 1e-5
NE = 9
F_FF = 2048

_DEBUG = []
_MAX_ON = None
_Y_TERMS = 3
_ON_TRACE = []
_N_PRE = 12
_N_OWN = 4
_DO_FFN = True
_LAST_RES = [None]


def _mk(x):
    return np.ascontiguousarray(x)


def build_program(n_mac_pre=12, n_mac_own=4, do_ffn=True, debug=()):
    nc = bass.Bass("TRN2", target_bir_lowering=False, detect_race_conditions=False)
    NMAC = n_mac_pre + n_mac_own
    NCH = NMAC * CPM
    T_WIN = NMAC * MAC

    d_x = nc.dram_tensor("xin", [T_WIN + 1, C], F32, kind="ExternalInput").ap()
    d_wk = nc.dram_tensor("W_k", [SL, 128, C], F32, kind="ExternalInput").ap()
    d_wv = nc.dram_tensor("W_v", [SL, 128, C], F32, kind="ExternalInput").ap()
    d_wr = nc.dram_tensor("W_r", [SL, 128, C], F32, kind="ExternalInput").ap()
    d_wg = nc.dram_tensor("W_g", [SL, 128, C], F32, kind="ExternalInput").ap()
    d_wo = nc.dram_tensor("W_o", [SL, 128, C], F32, kind="ExternalInput").ap()
    d_w1 = nc.dram_tensor("maa_w1", [SL, 128, 160], F32, kind="ExternalInput").ap()
    d_w2 = nc.dram_tensor("w2aug", [33, 5, C], F32, kind="ExternalInput").ap()
    d_d1 = nc.dram_tensor("dec_w1", [SL, 128, 64], F32, kind="ExternalInput").ap()
    d_d2 = nc.dram_tensor("dec_w2", [64, C], F32, kind="ExternalInput").ap()
    d_tv = nc.dram_tensor("tvec", [128, SL, 3], F32, kind="ExternalInput").ap()
    d_gb = nc.dram_tensor("lngb", [128, 2, C], F32, kind="ExternalInput").ap()
    d_rw = nc.dram_tensor("router_wT", [SL, 128, 8], F32, kind="ExternalInput").ap()
    d_rb = nc.dram_tensor("router_b", [8, 1], F32, kind="ExternalInput").ap()
    d_up = nc.dram_tensor("wupT", [NE, SL, 128, F_FF], F32, kind="ExternalInput").ap()
    d_dn = nc.dram_tensor("wdnT", [NE, 16, 128, C], F32, kind="ExternalInput").ap()
    d_out = nc.dram_tensor("out", [T_OWN, C], F32, kind="ExternalOutput").ap()

    dbg_out = {}
    for name, shape in debug:
        dbg_out[name] = nc.dram_tensor("dbg_" + name, list(shape), F32, kind="ExternalOutput").ap()

    i_idb = nc.inline_tensor(np.eye(128, dtype=BF), "idb")
    i_idf = nc.inline_tensor(np.eye(128, dtype=np.float32), "idf")
    i_mask = nc.inline_tensor(np.triu(np.ones((128, 128), dtype=BF), 1), "masksu")
    i_ones = nc.inline_tensor(np.ones((128, 64), dtype=BF), "ones64")

    with contextlib.ExitStack() as ctx:
        E = ctx.enter_context

        sems, cnt = {}, {}
        for s in ["c0", "x0", "x1", "xe0", "xe1", "wd", "ws", "t", "v", "s", "g", "st", "xf", "w0", "w1", "db"]:
            sems[s] = E(nc.semaphore("sem_" + s))
            cnt[s] = 0

        def w(eng, name, val=None):
            v = cnt[name] if val is None else val
            assert v <= cnt[name], ("forward wait", name, v, cnt[name])
            eng.wait_ge(sems[name], v)

        s_x1 = E(nc.sbuf_tensor("s_x1", [128, SL, T_OWN], BF16))
        s_id = E(nc.sbuf_tensor("s_id", [128, 128], BF16))
        s_mkk = E(nc.sbuf_tensor("s_mkk", [128, 128], BF16))
        s_o64 = E(nc.sbuf_tensor("s_o64", [128, 64], BF16))
        s_ep1 = E(nc.sbuf_tensor("s_ep1", [128, C], F32))
        s_ep2 = E(nc.sbuf_tensor("s_ep2", [128, C], BF16))
        s_zg = E(nc.sbuf_tensor("s_zg", [128, C], BF16))
        s_st3 = E(nc.sbuf_tensor("s_st3", [128, 4], F32))

        p_A = E(nc.psum_tensor("p_A", [128, 512], F32))
        p_B = E(nc.psum_tensor("p_B", [128, 512], F32))
        p_Y = E(nc.psum_tensor("p_Y", [128, 1024], F32))
        p_Z = E(nc.psum_tensor("p_Z", [128, 1024], F32))
        p_T = E(nc.psum_tensor("p_T", [128, 512], BF16))

        block = E(nc.Block())

        _onctr = [0]

        def on(engname, f):
            import sys as _sys
            _onctr[0] += 1
            _ON_TRACE.append((_onctr[0], engname, _sys._getframe(1).f_lineno))
            if _MAX_ON is not None and _onctr[0] > _MAX_ON:
                return
            getattr(block, engname)(f)

        inner = ctx.enter_context(contextlib.ExitStack())
        E2 = inner.enter_context

        s_wk = E2(nc.sbuf_tensor("s_wk", [128, SL, C], BF16))
        s_wv = E2(nc.sbuf_tensor("s_wv", [128, SL, C], BF16))
        s_ws = E2(nc.sbuf_tensor("s_ws", [128, SL, C], BF16))
        s_w1 = E2(nc.sbuf_tensor("s_w1", [128, SL, 160], BF16))
        s_w2 = E2(nc.sbuf_tensor("s_w2", [33, 5, C], BF16))
        s_d1 = E2(nc.sbuf_tensor("s_d1", [128, SL, 64], BF16))
        s_d2 = E2(nc.sbuf_tensor("s_d2", [64, C], BF16))
        s_tv = E2(nc.sbuf_tensor("s_tv", [128, SL, 3], F32))
        s_gb = E2(nc.sbuf_tensor("s_gb", [128, 2, C], BF16))
        s_n1 = E2(nc.sbuf_tensor("s_n1", [128, CH], F32))
        s_S = E2(nc.sbuf_tensor("s_S", [128, SL, HK], F32))
        s_Sc = E2(nc.sbuf_tensor("s_Sc", [128, SL, HK], BF16))

        s_xf = E2(nc.sbuf_tensor("s_xf", [128, SL, MAC], BF16))
        s_xx = E2(nc.sbuf_tensor("s_xx", [128, SL, MAC], BF16))
        s_mx = E2(nc.sbuf_tensor("s_mx", [128, SL, MAC], BF16))
        s_rf = E2(nc.sbuf_tensor("s_rf", [128, SL, MAC], BF16))
        s_kf = E2(nc.sbuf_tensor("s_kf", [128, SL, MAC], BF16))
        s_vt = E2(nc.sbuf_tensor("s_vt", [128, CPM, C], BF16))
        s_gg = E2(nc.sbuf_tensor("s_gg", [128, CPM, C], BF16))
        s_m5 = E2(nc.sbuf_tensor("s_m5", [33, 5, MAC], BF16))
        s_m6 = E2(nc.sbuf_tensor("s_m6", [64, MAC], BF16))
        s_pc = E2(nc.sbuf_tensor("s_pc", [128, SL, 1], BF16))

        s_xtm = E2(nc.sbuf_tensor("s_xtm", [128, 2, C], BF16))
        s_xep = E2(nc.sbuf_tensor("s_xep", [128, 2, C], BF16))
        s_ew = E2(nc.sbuf_tensor("s_ew", [128, SL, CH], F32))
        s_L = E2(nc.sbuf_tensor("s_L", [128, SL, CH + 1], F32))
        s_D1 = E2(nc.sbuf_tensor("s_D1", [128, SL, CH], BF16))
        s_D2 = E2(nc.sbuf_tensor("s_D2", [128, SL, CH], BF16))
        s_Rp = E2(nc.sbuf_tensor("s_Rp", [128, SL, CH], BF16))
        s_Kp = E2(nc.sbuf_tensor("s_Kp", [128, SL, CH], BF16))
        s_Kt = E2(nc.sbuf_tensor("s_Kt", [128, SL, CH], BF16))
        s_P2 = E2(nc.sbuf_tensor("s_P2", [128, SL, CH], BF16))
        s_M = E2(nc.sbuf_tensor("s_M", [128, 4, CH], BF16))
        s_ti = E2(nc.sbuf_tensor("s_ti", [128, SL, 2], F32))
        s_te = E2(nc.sbuf_tensor("s_te", [128, SL, 2], F32))
        s_zf = E2(nc.sbuf_tensor("s_zf", [128, SL, 128], BF16))
        s_st1 = E2(nc.sbuf_tensor("s_st1", [128, H], F32))
        s_st2 = E2(nc.sbuf_tensor("s_st2", [128, H], F32))
        s_dg = E2(nc.sbuf_tensor("s_dg", [128, H], F32))

        def f(sync):
            li = sync.dma_start(out=s_tv[:], in_=d_tv).then_inc(sems["c0"], 16)
            li = sync.dma_start(out=s_id[:], in_=i_idb.ap()).then_inc(sems["c0"], 16)
            li = sync.dma_start(out=s_mkk[:], in_=i_mask.ap()).then_inc(sems["c0"], 16)
            li = sync.dma_start(out=s_o64[:], in_=i_ones.ap()).then_inc(sems["c0"], 16)
        on("sync", f)
        cnt["c0"] += 4 * 16

        def f(gpsimd):
            for dst, src in [(s_wk, d_wk), (s_wv, d_wv), (s_w1, d_w1), (s_d1, d_d1)]:
                li = gpsimd.dma_start(out=dst[:], in_=src.rearrange("s p c -> p s c")).then_inc(sems["wd"], 16)
            li = gpsimd.dma_start(out=s_w2[:], in_=d_w2).then_inc(sems["wd"], 16)
            li = gpsimd.dma_start(out=s_d2[:], in_=d_d2).then_inc(sems["wd"], 16)
            li = gpsimd.dma_start(out=s_gb[:], in_=d_gb).then_inc(sems["wd"], 16)
            li = gpsimd.dma_start(out=s_pc[:], in_=d_x[0:1, :].rearrange("r (s p) -> p s r", p=128),
                             allow_slow_non_contiguous=True).then_inc(sems["wd"], 16)
        on("gpsimd", f)
        cnt["wd"] += 8 * 16

        def f(vector):
            w(vector, "c0")
            w(vector, "wd")
            li = vector.memset(s_n1[:], -1.0)
            li = vector.memset(s_S[:], 0.0)
            li = vector.memset(s_L[:, :, 0:1], 0.0)
            li = vector.memset(s_m5[32:33, :, :], 1.0)
            li.then_inc(sems["v"], 1)
        on("vector", f)
        cnt["v"] += 1
        v_init = cnt["v"]

        xw_after = {}

        def emit_xload(g):
            par = g % 2
            sname = "x%d" % par

            def f(gpsimd, g=g, par=par, sname=sname):
                if g >= 2:
                    w(gpsimd, "xf", g - 1)
                li = gpsimd.dma_start(out=s_xtm[:, par, :],
                                 in_=d_x[1 + g * CH: 1 + (g + 1) * CH, :]).then_inc(sems[sname], 16)
            on("gpsimd", f)
            cnt[sname] += 16
            xw_after[g] = cnt[sname]

        def dbg(name, ap_src):
            if name in dbg_out:
                def f(gpsimd):
                    for sn in ["t", "v", "s", "g"]:
                        w(gpsimd, sn)
                    li = gpsimd.dma_start(out=dbg_out[name], in_=ap_src).then_inc(sems["db"], 16)
                on("gpsimd", f)
                cnt["db"] += 16
                for eng in ["tensor", "vector", "scalar"]:
                    def f(e):
                        w(e, "db")
                    on(eng, f)

        emit_xload(0)
        emit_xload(1)

        TGT = ["w", "k", "v", "r", "g"]

        for m in range(NMAC):
            own = m >= n_mac_pre
            tgts = TGT if own else TGT[:3]

            if own:
                def f(gpsimd):
                    w(gpsimd, "t")
                    li = gpsimd.dma_start(out=s_ws[:], in_=d_wr.rearrange("s p c -> p s c")).then_inc(sems["ws"], 16)
                on("gpsimd", f)
                cnt["ws"] += 16
            ws_r = cnt["ws"]

            for cc in range(CPM):
                g = m * CPM + cc
                par = g % 2
                xname = "x%d" % par
                for half in range(2):
                    def f(tensor, g=g, par=par, half=half, xname=xname, m=m, cc=cc):
                        if m == 0 and cc == 0 and half == 0:
                            w(tensor, "v", v_init)
                        w(tensor, xname, xw_after[g])
                        w(tensor, "v")
                        if half == 0 and cc == 0:
                            w(tensor, "v")
                            w(tensor, "g")
                        for i in range(4):
                            s = half * 4 + i
                            li = tensor.transpose(p_T[:, i * 128:(i + 1) * 128],
                                             s_xtm[:, par, 128 * s:128 * (s + 1)], s_id[:])
                        li.then_inc(sems["t"], 1)
                    on("tensor", f)
                    cnt["t"] += 1
                    tval = cnt["t"]

                    def f(vector, cc=cc, half=half, tval=tval):
                        w(vector, "t", tval)
                        li = vector.tensor_copy(s_xf[:, half * 4:(half + 1) * 4, cc * CH:(cc + 1) * CH],
                                    p_T[:].rearrange("p (s c) -> p s c", s=4))
                        li.then_inc(sems["v"], 1)
                    on("vector", f)
                    cnt["v"] += 1
                s_xtr = cnt["v"]

                def f(gpsimd, s_xtr=s_xtr):
                    w(gpsimd, "v", s_xtr)
                    gpsimd.nop().then_inc(sems["xf"], 1)
                on("gpsimd", f)
                cnt["xf"] += 1
                if g + 2 < NCH:
                    emit_xload(g + 2)

            def f(vector):
                w(vector, "s")
                w(vector, "g")
                li = vector.tensor_sub(s_xx[:, :, 1:MAC], s_xf[:, :, 0:MAC - 1], s_xf[:, :, 1:MAC])
                li = vector.tensor_sub(s_xx[:, :, 0:1], s_pc[:], s_xf[:, :, 0:1])
                li.then_inc(sems["v"], 1)
                cnt["v"] += 1
                vector.wait_ge(sems["v"], cnt["v"])
                for s in range(SL):
                    li = vector.scalar_tensor_tensor(s_mx[:, s, :], s_xx[:, s, :], s_tv[:, s, 0:1],
                                                s_xf[:, s, :], ALU.mult, ALU.add)
                li = vector.tensor_copy(s_pc[:], s_xf[:, :, MAC - 1:MAC])
                li.then_inc(sems["v"], 1)
            on("vector", f)
            cnt["v"] += 1
            v_xxx = cnt["v"]

            for t_i, tgt in enumerate(TGT):
                if tgt not in tgts:
                    continue
                m5slot = [p_A[0:32, 0:MAC], p_B[0:32, 0:MAC],
                          p_Z[0:32, 0:512], p_Z[0:32, 512:1024]]

                def f(tensor, t_i=t_i, v_xxx=v_xxx):
                    w(tensor, "v", v_xxx)
                    if t_i == 4:
                        w(tensor, "s", m5_sc1)
                    for dc in range(SL):
                        li = tensor.matmul(m5slot[t_i % 4],
                                      s_w1[:, dc, 32 * t_i:32 * (t_i + 1)],
                                      s_mx[:, dc, :], start=(dc == 0), stop=(dc == SL - 1))
                    li.then_inc(sems["t"], 1)
                on("tensor", f)
                cnt["t"] += 1
                tval = cnt["t"]

                if t_i == 2:
                    def f(scalar, tval=tval, m5slot=m5slot):
                        w(scalar, "t", tval)
                        for j in range(3):
                            li = scalar.activation(s_m5[0:32, j, :], m5slot[j], AF.Tanh)
                        li.then_inc(sems["s"], 1)
                    on("scalar", f)
                    cnt["s"] += 1
                    m5_sc1 = cnt["s"]
                elif t_i == 4:
                    def f(scalar, tval=tval, m5slot=m5slot):
                        w(scalar, "t", tval)
                        for j in range(3, 5):
                            li = scalar.activation(s_m5[0:32, j, :], m5slot[j % 4], AF.Tanh)
                        li.then_inc(sems["s"], 1)
                    on("scalar", f)
                    cnt["s"] += 1
            s_m5d = cnt["s"]

            for t_i, tgt in enumerate(TGT):
                if tgt not in tgts:
                    continue
                if tgt == "g":
                    mixslot = [p_A[:, 0:MAC], p_Z[:, 0:512], p_Z[:, 512:1024]]
                else:
                    mixslot = [p_A[:, 0:MAC], p_B[:, 0:MAC], p_Z[:, 0:512], p_Z[:, 512:1024]]
                nslot = len(mixslot)
                v_hist = []
                for s in range(SL):
                    def f(tensor, t_i=t_i, s=s, v_hist=v_hist, nslot=nslot, mixslot=mixslot):
                        if s == 0:
                            w(tensor, "s")
                            w(tensor, "g")
                            w(tensor, "v")
                        elif s >= nslot:
                            w(tensor, "v", v_hist[s - nslot])
                        li = tensor.matmul(mixslot[s % nslot], s_w2[:, t_i, 128 * s:128 * (s + 1)],
                                      s_m5[:, t_i, :], start=True, stop=True)
                        li.then_inc(sems["t"], 1)
                    on("tensor", f)
                    cnt["t"] += 1
                    tval = cnt["t"]

                    def f(vector, s=s, tval=tval, nslot=nslot, mixslot=mixslot):
                        w(vector, "t", tval)
                        li = vector.tensor_mul(s_mx[:, s, :], mixslot[s % nslot], s_xx[:, s, :])
                        li.then_inc(sems["v"], 1)
                    on("vector", f)
                    cnt["v"] += 1
                    vval = cnt["v"]
                    v_hist.append(vval)

                    def f(gpsimd, s=s, vval=vval):
                        w(gpsimd, "v", vval)
                        li = gpsimd.tensor_add(s_mx[:, s, :], s_mx[:, s, :], s_xf[:, s, :])
                        li.then_inc(sems["g"], 1)
                    on("gpsimd", f)
                    cnt["g"] += 1
                g_mix = cnt["g"]

                if tgt == "w":
                    def f(tensor, g_mix=g_mix):
                        w(tensor, "g", g_mix)
                        w(tensor, "s")
                        for dc in range(SL):
                            li = tensor.matmul(p_A[0:64, 0:MAC], s_d1[:, dc, :], s_mx[:, dc, :],
                                          start=(dc == 0), stop=(dc == SL - 1))
                        li.then_inc(sems["t"], 1)
                    on("tensor", f)
                    cnt["t"] += 1
                    tval = cnt["t"]

                    def f(scalar, tval=tval):
                        w(scalar, "t", tval)
                        li = scalar.activation(s_m6[:], p_A[0:64, 0:MAC], AF.Tanh)
                        li.then_inc(sems["s"], 1)
                    on("scalar", f)
                    cnt["s"] += 1
                elif tgt in ("k", "r"):
                    dst = s_kf if tgt == "k" else s_rf
                    for o in range(SL):
                        def f(tensor, o=o, g_mix=g_mix, tgt=tgt):
                            w(tensor, "g", g_mix)
                            w(tensor, "v")
                            if tgt == "r" and o == 0:
                                w(tensor, "ws", ws_r)
                            wmat = s_wk if tgt == "k" else s_ws
                            pp = p_B if o % 2 == 0 else p_A
                            for dc in range(SL):
                                li = tensor.matmul(pp[:, 0:MAC], wmat[:, dc, 128 * o:128 * (o + 1)],
                                              s_mx[:, dc, :], start=(dc == 0), stop=(dc == SL - 1))
                            li.then_inc(sems["t"], 1)
                        on("tensor", f)
                        cnt["t"] += 1
                        tval = cnt["t"]

                        def f(vector, o=o, tval=tval, dst=dst):
                            w(vector, "t", tval)
                            li = vector.tensor_copy(dst[:, o, :], (p_B if o % 2 == 0 else p_A)[:, 0:MAC])
                            li.then_inc(sems["v"], 1)
                        on("vector", f)
                        cnt["v"] += 1
                else:
                    wm = s_wv if tgt == "v" else s_ws
                    dstt = s_vt if tgt == "v" else s_gg
                    if tgt == "g":
                        def f(gpsimd, g_mix=g_mix):
                            w(gpsimd, "t")
                            li = gpsimd.dma_start(out=s_ws[:], in_=d_wg.rearrange("s p c -> p s c")).then_inc(sems["ws"], 16)
                        on("gpsimd", f)
                        cnt["ws"] += 16
                    ws_tg = cnt["ws"]
                    for cc in range(CPM):
                        def f(tensor, cc=cc, g_mix=g_mix, wm=wm, tgt=tgt, ws_tg=ws_tg):
                            w(tensor, "g", g_mix)
                            if tgt == "g" and cc == 0:
                                w(tensor, "ws", ws_tg)
                            w(tensor, "v")
                            w(tensor, "s")
                            for dc in range(SL):
                                for hh in range(2):
                                    li = tensor.matmul(p_Y[:, hh * 512:(hh + 1) * 512],
                                                  s_mx[:, dc, cc * CH:(cc + 1) * CH],
                                                  wm[:, dc, hh * 512:(hh + 1) * 512],
                                                  start=(dc == 0), stop=(dc == SL - 1))
                            li.then_inc(sems["t"], 1)
                        on("tensor", f)
                        cnt["t"] += 1
                        tval = cnt["t"]
                        if tgt == "v":
                            def f(vector, cc=cc, tval=tval):
                                w(vector, "t", tval)
                                li = vector.tensor_copy(s_vt[:, cc, :], p_Y[:])
                                li.then_inc(sems["v"], 1)
                            on("vector", f)
                            cnt["v"] += 1
                        else:
                            def f(scalar, cc=cc, tval=tval):
                                w(scalar, "t", tval)
                                li = scalar.activation(s_gg[:, cc, :], p_Y[:], AF.Silu)
                                li.then_inc(sems["s"], 1)
                            on("scalar", f)
                            cnt["s"] += 1

            if own:
                def f(gpsimd):
                    w(gpsimd, "t")
                    li = gpsimd.dma_start(out=s_ws[:], in_=d_wo.rearrange("s p c -> p s c")).then_inc(sems["ws"], 16)
                on("gpsimd", f)
                cnt["ws"] += 16
            ws_o = cnt["ws"]

            dbg("xf_m%d" % m, s_xf[:])
            dbg("xx_m%d" % m, s_xx[:])
            dbg("rf_m%d" % m, s_rf[:])
            dbg("kf_m%d" % m, s_kf[:])
            dbg("vt_m%d" % m, s_vt[:])
            dbg("gg_m%d" % m, s_gg[:])
            dbg("m6_m%d" % m, s_m6[:])

            for cc in range(CPM):
                g = m * CPM + cc
                if own:
                    epar = cc % 2
                    ename = "xe%d" % epar

                    def f(gpsimd, g=g, epar=epar, ename=ename):
                        w(gpsimd, "v")
                        li = gpsimd.dma_start(out=s_xep[:, epar, :],
                                         in_=d_x[1 + g * CH:1 + (g + 1) * CH, :]).then_inc(sems[ename], 16)
                    on("gpsimd", f)
                    cnt[ename] += 16
                    xe_w = cnt[ename]

                for q in range(2):
                    def f(tensor, cc=cc, q=q):
                        w(tensor, "s")
                        w(tensor, "v")
                        pp = p_A if q == 0 else p_Z
                        for i in range(4):
                            s = q * 4 + i
                            li = tensor.matmul(pp[:, i * CH:(i + 1) * CH],
                                          s_d2[:, 128 * s:128 * (s + 1)],
                                          s_m6[:, cc * CH:(cc + 1) * CH],
                                          start=True, stop=True)
                        li.then_inc(sems["t"], 1)
                    on("tensor", f)
                    cnt["t"] += 1
                    tval = cnt["t"]

                    def f(vector, q=q, tval=tval):
                        w(vector, "t", tval)
                        pp = p_A if q == 0 else p_Z
                        li = vector.tensor_add(s_ew[:, q * 4:(q + 1) * 4, :],
                                          pp[:, 0:512].rearrange("p (s c) -> p s c", s=4),
                                          s_tv[:, q * 4:(q + 1) * 4, 1:2].broadcast_to([128, 4, CH]))
                        li.then_inc(sems["v"], 1)
                    on("vector", f)
                    cnt["v"] += 1
                    vval = cnt["v"]

                def f(scalar, vval=vval):
                    w(scalar, "v", vval)
                    li = scalar.activation(s_ew[:], s_ew[:], AF.Exp)
                    li.then_inc(sems["s"], 1)
                on("scalar", f)
                cnt["s"] += 1
                s_ewd = cnt["s"]

                def f(vector, s_ewd=s_ewd, own=own):
                    w(vector, "s", s_ewd)
                    for s in range(SL):
                        li = vector.tensor_tensor_scan(s_L[:, s, 1:CH + 1], s_ew[:, s, :],
                                                  s_n1[:], 0.0, ALU.subtract, ALU.mult)
                    li.then_inc(sems["v"], 1)
                    cnt["v"] += 1
                    vector.wait_ge(sems["v"], cnt["v"])
                    li = vector.tensor_scalar_mul(s_ti[:, :, 0:1], s_L[:, :, CH:CH + 1], 0.5)
                    li = vector.tensor_copy(s_ti[:, :, 1:2], s_L[:, :, CH:CH + 1])
                    li.then_inc(sems["v"], 1)
                    cnt["v"] += 1
                    vector.wait_ge(sems["v"], cnt["v"])
                    if own:
                        li = vector.tensor_sub(s_L[:, :, 1:CH + 1], s_L[:, :, 1:CH + 1],
                                          s_ti[:, :, 0:1].broadcast_to([128, SL, CH]))
                        li = vector.tensor_scalar_mul(s_L[:, :, 0:1], s_ti[:, :, 0:1], -1.0)
                    else:
                        li = vector.tensor_sub(s_L[:, :, 1:CH + 1],
                                          s_ti[:, :, 1:2].broadcast_to([128, SL, CH]),
                                          s_L[:, :, 1:CH + 1])
                    li.then_inc(sems["v"], 1)
                on("vector", f)
                cnt["v"] += 1
                v_scan = cnt["v"]
                dbg("ew_c%d" % g, s_ew[:])

                def f(scalar, own=own, v_scan=v_scan):
                    w(scalar, "v", v_scan)
                    li = scalar.activation(s_te[:], s_ti[:], AF.Exp)
                    if own:
                        li = scalar.activation(s_D1[:], s_L[:, :, 0:CH], AF.Exp)
                        li = scalar.activation(s_D2[:], s_L[:, :, 1:CH + 1], AF.Exp, scale=-1.0)
                    else:
                        li = scalar.activation(s_D2[:], s_L[:, :, 1:CH + 1], AF.Exp)
                    li.then_inc(sems["s"], 1)
                on("scalar", f)
                cnt["s"] += 1
                s_dd = cnt["s"]
                dbg("D2_c%d" % g, s_D2[:])
                dbg("te_c%d" % g, s_te[:])
                dbg("ti_c%d" % g, s_ti[:])
                dbg("La_c%d" % g, s_L[:])

                def f(vector, cc=cc, own=own, s_dd=s_dd):
                    w(vector, "s", s_dd)
                    w(vector, "t")
                    sl_tok = slice(cc * CH, (cc + 1) * CH)
                    li = vector.tensor_mul(s_Kp[:], s_kf[:, :, sl_tok], s_D2[:])
                    if own:
                        li = vector.tensor_mul(s_Rp[:], s_rf[:, :, sl_tok], s_D1[:])
                        for s in range(SL):
                            li = vector.scalar_tensor_tensor(s_P2[:, s, :], s_rf[:, s, sl_tok],
                                                        s_tv[:, s, 2:3], s_kf[:, s, sl_tok],
                                                        ALU.mult, ALU.mult)
                        li = vector.tensor_mul(s_Sc[:], s_S[:],
                                          s_te[:, :, 0:1].broadcast_to([128, SL, HK]))
                    li.then_inc(sems["v"], 1)
                on("vector", f)
                cnt["v"] += 1
                v_rk = cnt["v"]

                for half in range(2):
                    def f(tensor, half=half, v_rk=v_rk):
                        w(tensor, "v")
                        for i in range(4):
                            s = half * 4 + i
                            li = tensor.transpose(p_T[:, i * 128:(i + 1) * 128], s_Kp[:, s, :], s_id[:])
                        li.then_inc(sems["t"], 1)
                    on("tensor", f)
                    cnt["t"] += 1
                    tval = cnt["t"]

                    def f(vector, half=half, tval=tval):
                        w(vector, "t", tval)
                        li = vector.tensor_copy(s_Kt[:, half * 4:(half + 1) * 4, :],
                                    p_T[:].rearrange("p (s c) -> p s c", s=4))
                        li.then_inc(sems["v"], 1)
                    on("vector", f)
                    cnt["v"] += 1
                s_ktd = cnt["v"]

                t_yall = None
                if own:
                    for grp in range(4):
                        heads = [(grp % 2) + 8 * (grp // 2) + 2 * i for i in range(4)]
                        p_M = p_A if grp % 2 == 0 else p_B

                        def f(tensor, grp=grp, heads=heads, p_M=p_M):
                            w(tensor, "v")
                            for i, h in enumerate(heads):
                                s, pb = h // 2, (h % 2) * 64
                                li = tensor.matmul(p_M[:, i * CH:(i + 1) * CH],
                                              s_Kp[pb:pb + 64, s, :], s_Rp[pb:pb + 64, s, :],
                                              start=True, stop=True)
                            li.then_inc(sems["t"], 1)
                        on("tensor", f)
                        cnt["t"] += 1
                        tval = cnt["t"]

                        def f(vector, tval=tval, p_M=p_M):
                            w(vector, "t", tval)
                            li = vector.tensor_mul(s_M[:], p_M[:].rearrange("p (q c) -> p q c", q=4),
                                              s_mkk[:].rearrange("p (q c) -> p q c", q=1).broadcast_to([128, 4, CH]))
                            li.then_inc(sems["v"], 1)
                        on("vector", f)
                        cnt["v"] += 1
                        vval = cnt["v"]

                        def f(tensor, grp=grp, cc=cc, vval=vval, s_ktd=s_ktd, heads=heads):
                            w(tensor, "v", vval)
                            w(tensor, "v", s_ktd)
                            for i, h in enumerate(heads):
                                s, pb = h // 2, (h % 2) * 64
                                co = h * 64
                                li = tensor.matmul(p_Y[:, co:co + 64], s_M[:, i, :],
                                              s_vt[:, cc, co:co + 64], start=True, stop=False)
                                li = tensor.matmul(p_Y[:, co:co + 64], s_Rp[pb:pb + 64, s, :],
                                              s_Sc[pb:pb + 64, s, :], start=False, stop=True)
                            li.then_inc(sems["t"], 1)
                        on("tensor", f)
                        cnt["t"] += 1
                    t_yall = cnt["t"]

                def f(tensor, cc=cc, s_ktd=s_ktd):
                    w(tensor, "v")
                    for s in range(SL):
                        for hh in range(2):
                            li = tensor.matmul(p_B[hh * 64:hh * 64 + 64, s * 64:(s + 1) * 64],
                                          s_Kt[:, s, hh * 64:(hh + 1) * 64],
                                          s_vt[:, cc, (2 * s + hh) * 64:(2 * s + hh + 1) * 64],
                                          start=True, stop=True)
                    li.then_inc(sems["t"], 1)
                on("tensor", f)
                cnt["t"] += 1
                t_state = cnt["t"]

                def f(vector, own=own, t_state=t_state):
                    w(vector, "t", t_state)
                    e1a = s_ep1[:, 0:512].rearrange("p (s j) -> p s j", s=SL)
                    e1b = s_ep1[:, 512:1024].rearrange("p (s j) -> p s j", s=SL)
                    li = vector.tensor_mul(e1a, s_S[:], s_te[:, :, 1:2].broadcast_to([128, SL, HK]))
                    if own:
                        li = vector.tensor_mul(e1b, p_B[:].rearrange("p (s j) -> p s j", s=SL),
                                          s_te[:, :, 0:1].broadcast_to([128, SL, HK]))
                        li.then_inc(sems["v"], 1)
                        cnt["v"] += 1
                        vector.wait_ge(sems["v"], cnt["v"])
                        li = vector.tensor_add(s_S[:], e1a, e1b)
                    else:
                        li = vector.tensor_add(s_S[:], e1a, p_B[:].rearrange("p (s j) -> p s j", s=SL))
                    li.then_inc(sems["v"], 1)
                on("vector", f)
                cnt["v"] += 1

                dbg("L_c%d" % g, s_L[:])
                dbg("S_c%d" % g, s_S[:])

                if not own:
                    continue

                for half in range(2):
                    def f(tensor, half=half, t_yall=t_yall):
                        w(tensor, "s")
                        w(tensor, "v")
                        for i in range(4):
                            s = half * 4 + i
                            li = tensor.transpose(p_T[:, i * 128:(i + 1) * 128], s_P2[:, s, :], s_id[:])
                        li.then_inc(sems["t"], 1)
                    on("tensor", f)
                    cnt["t"] += 1
                    tval = cnt["t"]

                    def f(vector, half=half, tval=tval):
                        w(vector, "t", tval)
                        li = vector.tensor_copy(s_Kt[:, half * 4:(half + 1) * 4, :],
                                    p_T[:].rearrange("p (s c) -> p s c", s=4))
                        li.then_inc(sems["v"], 1)
                    on("vector", f)
                    cnt["v"] += 1
                s_p2t = cnt["v"]

                def f(vector, cc=cc, t_yall=t_yall, s_p2t=s_p2t):
                    w(vector, "t", t_yall)
                    li = vector.tensor_reduce(s_dg[:].rearrange("p (h o) -> p h o", o=1),
                                         s_Kt[:].rearrange("p s c -> p (s c)").rearrange("p (h j) -> p h j", h=H),
                                         AXX, ALU.add)
                    li.then_inc(sems["v"], 1)
                    cnt["v"] += 1
                    vector.wait_ge(sems["v"], cnt["v"])
                    li = vector.tensor_mul(s_zg[:].rearrange("p (h j) -> p h j", h=H),
                                      s_vt[:, cc, :].rearrange("p (h j) -> p h j", h=H),
                                      s_dg[:].rearrange("p (h o) -> p h o", o=1).broadcast_to([128, H, HK]))
                    li.then_inc(sems["v"], 1)
                    cnt["v"] += 1
                    vector.wait_ge(sems["v"], cnt["v"])
                    li = vector.tensor_add(s_ep2[:], p_Y[:], s_zg[:])
                    li.then_inc(sems["v"], 1)
                on("vector", f)
                cnt["v"] += 1
                if ("y_c%d" % g) in dbg_out:
                    dbg("y_c%d" % g, s_ep2[:])

                def f(vector):
                    yv = s_ep2[:].rearrange("p (h j) -> p h j", h=H)
                    li = vector.tensor_reduce(s_st1[:].rearrange("p (h o) -> p h o", o=1), yv, AXX, ALU.add)
                    li.then_inc(sems["v"], 1)
                    cnt["v"] += 1
                    vector.wait_ge(sems["v"], cnt["v"])
                    li = vector.tensor_scalar_mul(s_st1[:], s_st1[:], 1.0 / HK)
                    li.then_inc(sems["v"], 1)
                    cnt["v"] += 1
                    vector.wait_ge(sems["v"], cnt["v"])
                    li = vector.tensor_sub(s_ep1[:].rearrange("p (h j) -> p h j", h=H), yv,
                                      s_st1[:].rearrange("p (h o) -> p h o", o=1).broadcast_to([128, H, HK]))
                    li.then_inc(sems["v"], 1)
                on("vector", f)
                cnt["v"] += 1
                vval = cnt["v"]
                dbg("yc_c%d" % g, s_ep1[:])

                def f(vector, vval=vval):
                    li = vector.tensor_mul(s_zg[:], s_ep1[:], s_ep1[:])
                    li.then_inc(sems["v"], 1)
                    cnt["v"] += 1
                    vector.wait_ge(sems["v"], cnt["v"])
                    li = vector.tensor_reduce(s_st2[:].rearrange("p (h o) -> p h o", o=1),
                                         s_zg[:].rearrange("p (h j) -> p h j", h=H), AXX, ALU.add)
                    li.then_inc(sems["v"], 1)
                    cnt["v"] += 1
                    vector.wait_ge(sems["v"], cnt["v"])
                    li = vector.tensor_scalar(s_st2[:], s_st2[:], 1.0 / HK, EPS_GN, ALU.mult, ALU.add)
                    li.then_inc(sems["v"], 1)
                on("vector", f)
                cnt["v"] += 1
                vval = cnt["v"]
                dbg("st1_c%d" % g, s_st1[:])
                dbg("var_c%d" % g, s_st2[:])

                def f(scalar, vval=vval):
                    w(scalar, "v", vval)
                    li = scalar.activation(s_st2[:], s_st2[:], AF.Sqrt)
                    li.then_inc(sems["s"], 1)
                on("scalar", f)
                cnt["s"] += 1
                sval = cnt["s"]
                dbg("rstd_c%d" % g, s_st2[:])

                def f(vector, cc=cc, sval=sval):
                    w(vector, "s", sval)
                    li = vector.reciprocal(s_st2[:], s_st2[:])
                    li.then_inc(sems["v"], 1)
                    cnt["v"] += 1
                    vector.wait_ge(sems["v"], cnt["v"])
                    li = vector.tensor_mul(s_ep1[:].rearrange("p (h j) -> p h j", h=H),
                                      s_ep1[:].rearrange("p (h j) -> p h j", h=H),
                                      s_st2[:].rearrange("p (h o) -> p h o", o=1).broadcast_to([128, H, HK]))
                    li = vector.tensor_mul(s_ep1[:], s_ep1[:], s_gb[:, 0, :])
                    li = vector.tensor_add(s_ep1[:], s_ep1[:], s_gb[:, 1, :])
                    li = vector.tensor_mul(s_zg[:], s_ep1[:], s_gg[:, cc, :])
                    li.then_inc(sems["v"], 1)
                on("vector", f)
                cnt["v"] += 1
                v_zg = cnt["v"]
                dbg("zg1_c%d" % g, s_zg[:])

                for half in range(2):
                    def f(tensor, half=half, v_zg=v_zg):
                        w(tensor, "v")
                        for i in range(4):
                            s = half * 4 + i
                            li = tensor.transpose(p_T[:, i * 128:(i + 1) * 128],
                                             s_zg[:, 128 * s:128 * (s + 1)], s_id[:])
                        li.then_inc(sems["t"], 1)
                    on("tensor", f)
                    cnt["t"] += 1
                    tval = cnt["t"]

                    def f(vector, half=half, tval=tval):
                        w(vector, "t", tval)
                        li = vector.tensor_copy(s_zf[:, half * 4:(half + 1) * 4, :],
                                    p_T[:].rearrange("p (s c) -> p s c", s=4))
                        li.then_inc(sems["v"], 1)
                    on("vector", f)
                    cnt["v"] += 1
                s_zfd = cnt["v"]

                def f(tensor, s_zfd=s_zfd, ws_o=ws_o):
                    w(tensor, "v", s_zfd)
                    w(tensor, "ws", ws_o)
                    w(tensor, "v")
                    for dc in range(SL):
                        for hh in range(2):
                            li = tensor.matmul(p_Z[:, hh * 512:(hh + 1) * 512], s_zf[:, dc, :],
                                          s_ws[:, dc, hh * 512:(hh + 1) * 512],
                                          start=(dc == 0), stop=(dc == SL - 1))
                    li.then_inc(sems["t"], 1)
                on("tensor", f)
                cnt["t"] += 1
                t_att = cnt["t"]

                def f(vector, cc=cc, t_att=t_att, xe_w=xe_w, ename=ename, epar=epar):
                    w(vector, "t", t_att)
                    w(vector, ename, xe_w)
                    li = vector.tensor_add(s_ep1[:], p_Z[:], s_xep[:, epar, :])
                    li.then_inc(sems["v"], 1)
                on("vector", f)
                cnt["v"] += 1
                vval = cnt["v"]
                dbg("res_c%d" % g, s_ep1[:])

                def f(vector, vval=vval):
                    li = vector.tensor_reduce(s_st3[:, 0:1], s_ep1[:], AXX, ALU.add)
                    li.then_inc(sems["v"], 1)
                    cnt["v"] += 1
                    vector.wait_ge(sems["v"], cnt["v"])
                    li = vector.tensor_scalar_mul(s_st3[:, 0:1], s_st3[:, 0:1], 1.0 / C)
                    li.then_inc(sems["v"], 1)
                    cnt["v"] += 1
                    vector.wait_ge(sems["v"], cnt["v"])
                    li = vector.tensor_scalar_sub(s_ep1[:], s_ep1[:], s_st3[:, 0:1])
                    li.then_inc(sems["v"], 1)
                    cnt["v"] += 1
                    vector.wait_ge(sems["v"], cnt["v"])
                    li = vector.tensor_mul(s_ep2[:], s_ep1[:], s_ep1[:])
                    li.then_inc(sems["v"], 1)
                    cnt["v"] += 1
                    vector.wait_ge(sems["v"], cnt["v"])
                    li = vector.tensor_reduce(s_st3[:, 1:2], s_ep2[:], AXX, ALU.add)
                    li.then_inc(sems["v"], 1)
                    cnt["v"] += 1
                    vector.wait_ge(sems["v"], cnt["v"])
                    li = vector.tensor_scalar(s_st3[:, 1:2], s_st3[:, 1:2], 1.0 / C, EPS_LN, ALU.mult, ALU.add)
                    li.then_inc(sems["v"], 1)
                on("vector", f)
                cnt["v"] += 1
                vval = cnt["v"]

                def f(scalar, vval=vval):
                    w(scalar, "v", vval)
                    li = scalar.activation(s_st3[:, 1:2], s_st3[:, 1:2], AF.Sqrt)
                    li.then_inc(sems["s"], 1)
                on("scalar", f)
                cnt["s"] += 1
                sval = cnt["s"]

                def f(vector, sval=sval):
                    w(vector, "s", sval)
                    li = vector.reciprocal(s_st3[:, 1:2], s_st3[:, 1:2])
                    li.then_inc(sems["v"], 1)
                    cnt["v"] += 1
                    vector.wait_ge(sems["v"], cnt["v"])
                    li = vector.tensor_scalar_mul(s_zg[:], s_ep1[:], s_st3[:, 1:2])
                    li.then_inc(sems["v"], 1)
                on("vector", f)
                cnt["v"] += 1
                v_x1 = cnt["v"]

                tok0 = (m - n_mac_pre) * MAC + cc * CH
                for half in range(2):
                    def f(tensor, half=half, v_x1=v_x1):
                        w(tensor, "v")
                        for i in range(4):
                            s = half * 4 + i
                            li = tensor.transpose(p_T[:, i * 128:(i + 1) * 128],
                                             s_zg[:, 128 * s:128 * (s + 1)], s_id[:])
                        li.then_inc(sems["t"], 1)
                    on("tensor", f)
                    cnt["t"] += 1
                    tval = cnt["t"]

                    def f(vector, half=half, tval=tval, tok0=tok0):
                        w(vector, "t", tval)
                        li = vector.tensor_copy(s_x1[:, half * 4:(half + 1) * 4, tok0:tok0 + CH],
                                    p_T[:].rearrange("p (s c) -> p s c", s=4))
                        li.then_inc(sems["v"], 1)
                    on("vector", f)
                    cnt["v"] += 1

                dbg("x1_c%d" % g, s_x1[:, :, tok0:tok0 + CH])

        dbg("x1", s_x1[:])

        inner.close()

        if do_ffn:
            NT128 = (n_mac_own * MAC) // 128
            NT512 = (n_mac_own * MAC) // 512
            s_acc = E(nc.sbuf_tensor("s_acc", [128, NT128, C], F32))
            s_wu = E(nc.sbuf_tensor("s_wu", [128, 2, SL, 1024], BF16))
            s_wd2 = E(nc.sbuf_tensor("s_wd2", [128, 2, SL, C], BF16))
            s_h = E(nc.sbuf_tensor("s_h", [128, SL, 512], BF16))
            s_cmb = E(nc.sbuf_tensor("s_cmb", [128, NT128, 8], F32))
            s_lg = E(nc.sbuf_tensor("s_lg", [128, NT128, 8], F32))
            s_q = E(nc.sbuf_tensor("s_q", [128, NT128, 8], F32))
            s_r8 = E(nc.sbuf_tensor("s_r8", [128, NT128, 4], F32))
            s_lgf = E(nc.sbuf_tensor("s_lgf", [8, 512], F32))
            s_rw = E(nc.sbuf_tensor("s_rw", [128, SL, 8], BF16))
            s_rb = E(nc.sbuf_tensor("s_rb", [8, 1], F32))
            s_if = E(nc.sbuf_tensor("s_if", [128, 128], F32))
            s_out = E(nc.sbuf_tensor("s_out", [128, 2, C], F32))

            bar = {k: cnt[k] for k in ["t", "v", "s", "g"]}
            for eng in ["tensor", "vector", "scalar", "gpsimd", "sync"]:
                def f(e, bar=bar):
                    for k, vv in bar.items():
                        w(e, k, vv)
                on(eng, f)

            def f(sync):
                li = sync.dma_start(out=s_rb[:], in_=d_rb).then_inc(sems["c0"], 16)
                li = sync.dma_start(out=s_if[:], in_=i_idf.ap()).then_inc(sems["c0"], 16)
            on("sync", f)
            cnt["c0"] += 32

            def f(gpsimd):
                li = gpsimd.dma_start(out=s_rw[:], in_=d_rw.rearrange("s p c -> p s c")).then_inc(sems["wd"], 16)
            on("gpsimd", f)
            cnt["wd"] += 16
            x0_ffn = cnt["c0"]
            wd_ffn = cnt["wd"]

            def f(vector):
                w(vector, "c0", x0_ffn)
                w(vector, "wd", wd_ffn)
                li = vector.memset(s_acc[:], 0.0)
                li.then_inc(sems["v"], 1)
            on("vector", f)
            cnt["v"] += 1

            for tt in range(NT512):
                def f(tensor, tt=tt):
                    w(tensor, "v")
                    for dc in range(SL):
                        li = tensor.matmul(p_A[0:8, 0:512], s_rw[:, dc, :],
                                      s_x1[:, dc, tt * 512:(tt + 1) * 512],
                                      start=(dc == 0), stop=(dc == SL - 1))
                    li.then_inc(sems["t"], 1)
                on("tensor", f)
                cnt["t"] += 1
                tval = cnt["t"]

                def f(vector, tval=tval):
                    w(vector, "t", tval)
                    li = vector.tensor_scalar_add(s_lgf[:], p_A[0:8, 0:512], s_rb[:])
                    li.then_inc(sems["v"], 1)
                on("vector", f)
                cnt["v"] += 1
                vval = cnt["v"]

                def f(tensor, vval=vval):
                    w(tensor, "v", vval)
                    for i in range(4):
                        li = tensor.transpose(p_B[:, i * 8:(i + 1) * 8],
                                         s_lgf[:, i * 128:(i + 1) * 128], s_if[0:8, 0:8])
                    li.then_inc(sems["t"], 1)
                on("tensor", f)
                cnt["t"] += 1
                tval = cnt["t"]

                def f(vector, tt=tt, tval=tval):
                    w(vector, "t", tval)
                    li = vector.tensor_copy(s_lg[:, tt * 4:(tt + 1) * 4, :],
                                       p_B[:, 0:32].rearrange("p (c e) -> p c e", e=8))
                    li.then_inc(sems["v"], 1)
                on("vector", f)
                cnt["v"] += 1

            def f(vector):
                li = vector.tensor_reduce(s_r8[:, :, 0:1], s_lg[:], AXX, ALU.max)
                li.then_inc(sems["v"], 1)
                cnt["v"] += 1
                vector.wait_ge(sems["v"], cnt["v"])
                li = vector.tensor_sub(s_q[:], s_lg[:], s_r8[:, :, 0:1].broadcast_to([128, NT128, 8]))
                li.then_inc(sems["v"], 1)
            on("vector", f)
            cnt["v"] += 1
            vval = cnt["v"]

            def f(scalar, vval=vval):
                w(scalar, "v", vval)
                li = scalar.activation(s_q[:], s_q[:], AF.Exp)
                li.then_inc(sems["s"], 1)
            on("scalar", f)
            cnt["s"] += 1
            sval = cnt["s"]

            def f(vector, sval=sval):
                w(vector, "s", sval)
                li = vector.tensor_single_scalar(s_cmb[:], s_q[:], 1.0, ALU.is_lt)
                li.then_inc(sems["v"], 1)
                cnt["v"] += 1
                vector.wait_ge(sems["v"], cnt["v"])
                li = vector.tensor_mul(s_cmb[:], s_cmb[:], s_q[:])
                li.then_inc(sems["v"], 1)
                cnt["v"] += 1
                vector.wait_ge(sems["v"], cnt["v"])
                li = vector.tensor_reduce(s_r8[:, :, 1:2], s_cmb[:], AXX, ALU.max)
                li.then_inc(sems["v"], 1)
                cnt["v"] += 1
                vector.wait_ge(sems["v"], cnt["v"])
                li = vector.tensor_scalar_add(s_r8[:, :, 2:3], s_r8[:, :, 1:2], 1.0)
                li.then_inc(sems["v"], 1)
                cnt["v"] += 1
                vector.wait_ge(sems["v"], cnt["v"])
                li = vector.reciprocal(s_r8[:, :, 3:4], s_r8[:, :, 2:3])
                li.then_inc(sems["v"], 1)
                cnt["v"] += 1
                vector.wait_ge(sems["v"], cnt["v"])
                li = vector.tensor_tensor(s_lg[:], s_q[:], s_r8[:, :, 1:2].broadcast_to([128, NT128, 8]), ALU.is_ge)
                li.then_inc(sems["v"], 1)
                cnt["v"] += 1
                vector.wait_ge(sems["v"], cnt["v"])
                li = vector.tensor_mul(s_q[:], s_q[:], s_lg[:])
                li.then_inc(sems["v"], 1)
                cnt["v"] += 1
                vector.wait_ge(sems["v"], cnt["v"])
                li = vector.tensor_mul(s_cmb[:], s_q[:], s_r8[:, :, 3:4].broadcast_to([128, NT128, 8]))
                li.then_inc(sems["v"], 1)
            on("vector", f)
            cnt["v"] += 1
            dbg("combine", s_cmb[:])

            tdone = []
            for eidx in range(NE):
                for fh in range(2):
                    idx = eidx * 2 + fh
                    ring = idx % 2
                    wsem = "w%d" % ring

                    def f(gpsimd, eidx=eidx, fh=fh, ring=ring, wsem=wsem, idx=idx):
                        if idx >= 2:
                            w(gpsimd, "t", tdone[idx - 2])
                        li = gpsimd.dma_start(out=s_wu[:, ring, :, :],
                                         in_=d_up[eidx, :, :, fh * 1024:(fh + 1) * 1024].rearrange("s p f -> p s f")
                                         ).then_inc(sems[wsem], 16)
                        li = gpsimd.dma_start(out=s_wd2[:, ring, :, :],
                                         in_=d_dn[eidx, fh * 8:(fh + 1) * 8, :, :].rearrange("s p f -> p s f")
                                         ).then_inc(sems[wsem], 16)
                    on("gpsimd", f)
                    cnt[wsem] += 32
                    wv = cnt[wsem]

                    for tt in range(NT512):
                        for fp in range(4):
                            def f(tensor, ring=ring, tt=tt, fp=fp, wsem=wsem, wv=wv):
                                if fp == 0 and tt == 0:
                                    w(tensor, wsem, wv)
                                w(tensor, "s")
                                for pp, fc in ((p_A, 2 * fp), (p_B, 2 * fp + 1)):
                                    for dc in range(SL):
                                        li = tensor.matmul(pp[:, 0:512], s_wu[:, ring, dc, fc * 128:(fc + 1) * 128],
                                                      s_x1[:, dc, tt * 512:(tt + 1) * 512],
                                                      start=(dc == 0), stop=(dc == SL - 1))
                                li.then_inc(sems["t"], 1)
                            on("tensor", f)
                            cnt["t"] += 1
                            tval = cnt["t"]

                            def f(scalar, fp=fp, tval=tval):
                                w(scalar, "t", tval)
                                li = scalar.activation(s_h[:, 2 * fp, :], p_A[:, 0:512], AF.Silu)
                                li = scalar.activation(s_h[:, 2 * fp + 1, :], p_B[:, 0:512], AF.Silu)
                                li.then_inc(sems["s"], 1)
                            on("scalar", f)
                            cnt["s"] += 1
                        s_hd = cnt["s"]

                        for st in range(4):
                            t128 = tt * 4 + st
                            pd = p_Y if st % 2 == 0 else p_Z

                            def f(tensor, ring=ring, st=st, s_hd=s_hd, pd=pd):
                                w(tensor, "s", s_hd)
                                w(tensor, "v")
                                for fc in range(SL):
                                    for hh in range(2):
                                        li = tensor.matmul(pd[:, hh * 512:(hh + 1) * 512],
                                                      s_h[:, fc, st * 128:(st + 1) * 128],
                                                      s_wd2[:, ring, fc, hh * 512:(hh + 1) * 512],
                                                      start=(fc == 0), stop=(fc == SL - 1))
                                li.then_inc(sems["t"], 1)
                            on("tensor", f)
                            cnt["t"] += 1
                            tval = cnt["t"]

                            def f(vector, eidx=eidx, t128=t128, tval=tval, pd=pd):
                                w(vector, "t", tval)
                                if eidx < 8:
                                    li = vector.scalar_tensor_tensor(s_acc[:, t128, :], pd[:],
                                                                s_cmb[:, t128, eidx:eidx + 1],
                                                                s_acc[:, t128, :], ALU.mult, ALU.add)
                                else:
                                    li = vector.tensor_add(s_acc[:, t128, :], pd[:], s_acc[:, t128, :])
                                li.then_inc(sems["v"], 1)
                            on("vector", f)
                            cnt["v"] += 1
                    tdone.append(cnt["t"])

            dbg("ffnacc", s_acc[:, 0, :])
            for t128 in range(NT128):
                for half in range(2):
                    def f(tensor, t128=t128, half=half):
                        w(tensor, "v")
                        for i in range(4):
                            s = half * 4 + i
                            li = tensor.transpose(p_T[:, i * 128:(i + 1) * 128],
                                             s_x1[:, s, t128 * 128:(t128 + 1) * 128], s_id[:])
                        li.then_inc(sems["t"], 1)
                    on("tensor", f)
                    cnt["t"] += 1
                    tval = cnt["t"]

                    def f(vector, half=half, tval=tval):
                        w(vector, "t", tval)
                        li = vector.tensor_copy(s_zg[:, half * 512:(half + 1) * 512], p_T[:])
                        li.then_inc(sems["v"], 1)
                    on("vector", f)
                    cnt["v"] += 1

                if t128 == 0:
                    dbg("x1tm0", s_zg[:])

                if t128 == 0:
                    dbg("fzg0", s_zg[:])

                def f(vector, t128=t128):
                    w(vector, "s")
                    li = vector.tensor_add(s_ep1[:], s_acc[:, t128, :], s_zg[:])
                    li.then_inc(sems["v"], 1)
                on("vector", f)
                cnt["v"] += 1
                vval = cnt["v"]
                if t128 == 0:
                    dbg("fres0", s_ep1[:])
                    dbg("facc0", s_acc[:, 0, :])

                def f(vector, vval=vval):
                    w(vector, "st")
                    li = vector.tensor_reduce(s_st3[:, 0:1], s_ep1[:], AXX, ALU.add)
                    li.then_inc(sems["v"], 1)
                    cnt["v"] += 1
                    vector.wait_ge(sems["v"], cnt["v"])
                    li = vector.tensor_scalar_mul(s_st3[:, 0:1], s_st3[:, 0:1], 1.0 / C)
                    li.then_inc(sems["v"], 1)
                    cnt["v"] += 1
                    vector.wait_ge(sems["v"], cnt["v"])
                    li = vector.tensor_scalar_sub(s_ep1[:], s_ep1[:], s_st3[:, 0:1])
                    li.then_inc(sems["v"], 1)
                    cnt["v"] += 1
                    vector.wait_ge(sems["v"], cnt["v"])
                    li = vector.tensor_mul(s_ep2[:], s_ep1[:], s_ep1[:])
                    li.then_inc(sems["v"], 1)
                    cnt["v"] += 1
                    vector.wait_ge(sems["v"], cnt["v"])
                    li = vector.tensor_reduce(s_st3[:, 1:2], s_ep2[:], AXX, ALU.add)
                    li.then_inc(sems["v"], 1)
                    cnt["v"] += 1
                    vector.wait_ge(sems["v"], cnt["v"])
                    li = vector.tensor_scalar(s_st3[:, 1:2], s_st3[:, 1:2], 1.0 / C, EPS_LN, ALU.mult, ALU.add)
                    li.then_inc(sems["v"], 1)
                on("vector", f)
                cnt["v"] += 1
                vval = cnt["v"]

                def f(scalar, vval=vval):
                    w(scalar, "v", vval)
                    li = scalar.activation(s_st3[:, 1:2], s_st3[:, 1:2], AF.Sqrt)
                    li.then_inc(sems["s"], 1)
                on("scalar", f)
                cnt["s"] += 1
                sval = cnt["s"]

                def f(vector, t128=t128, sval=sval):
                    w(vector, "s", sval)
                    li = vector.reciprocal(s_st3[:, 1:2], s_st3[:, 1:2])
                    li.then_inc(sems["v"], 1)
                    cnt["v"] += 1
                    vector.wait_ge(sems["v"], cnt["v"])
                    li = vector.tensor_scalar_mul(s_out[:, t128 % 2, :], s_ep1[:], s_st3[:, 1:2])
                    li.then_inc(sems["v"], 1)
                on("vector", f)
                cnt["v"] += 1
                vval = cnt["v"]
                if t128 == 0:
                    dbg("fout0", s_out[:, 0, :])
                    dbg("fst3", s_st3[:])

                def f(sync, t128=t128, vval=vval):
                    w(sync, "v", vval)
                    li = sync.dma_start(out=d_out[t128 * 128:(t128 + 1) * 128, :],
                                   in_=s_out[:, t128 % 2, :]).then_inc(sems["st"], 16)
                on("sync", f)
                cnt["st"] += 16

        def f(sync):
            w(sync, "st")
        on("sync", f)

        def f(gpsimd):
            w(gpsimd, "db")
        on("gpsimd", f)

    return nc



def _prep_shared(inputs):
    A = lambda k: np.asarray(inputs[k], dtype=np.float32)
    shared = {}
    for nm, key in [("W_r", "L0_attn_W_r"), ("W_k", "L0_attn_W_k"), ("W_v", "L0_attn_W_v"),
                    ("W_g", "L0_attn_W_g"), ("W_o", "L0_attn_W_o")]:
        shared[nm] = _mk(A(key).reshape(SL, 128, C))
    shared["maa_w1"] = _mk(A("L0_attn_time_maa_w1").reshape(SL, 128, 160))
    tm = [A("L0_attn_time_maa_w"), A("L0_attn_time_maa_k"), A("L0_attn_time_maa_v"),
          A("L0_attn_time_maa_r"), A("L0_attn_time_maa_g")]
    w2 = A("L0_attn_time_maa_w2")
    shared["w2aug"] = _mk(np.concatenate([w2, np.stack(tm)[:, None, :]], axis=1).transpose(1, 0, 2))
    shared["dec_w1"] = _mk(A("L0_attn_time_decay_w1").reshape(SL, 128, 64))
    shared["dec_w2"] = _mk(A("L0_attn_time_decay_w2"))
    tvec = np.stack([A("L0_attn_time_maa_x"), A("L0_attn_time_decay"),
                     A("L0_attn_time_faaaa").reshape(-1)], axis=-1)
    shared["tvec"] = _mk(tvec.reshape(SL, 128, 3).transpose(1, 0, 2))
    gb = np.stack([A("L0_attn_ln_gamma"), A("L0_attn_ln_beta")])
    shared["lngb"] = _mk(np.broadcast_to(gb[None], (128, 2, C)))
    shared["router_wT"] = _mk(A("L0_ffn_router_w").T.reshape(SL, 128, 8))
    shared["router_b"] = _mk(A("L0_ffn_router_b").reshape(8, 1))
    up = np.concatenate([A("L0_ffn_w_up"), A("L0_ffn_shared_w_up")], axis=0)
    dn = np.concatenate([A("L0_ffn_w_down"), A("L0_ffn_shared_w_down")], axis=0)
    shared["wupT"] = _mk(up.transpose(0, 2, 1).reshape(NE, SL, 128, F_FF))
    shared["wdnT"] = _mk(dn.transpose(0, 2, 1).reshape(NE, 16, 128, C))
    return shared


def _core_x(xfull, core, t_win):
    s, p = core // 4, core % 4
    t0 = p * T_OWN
    lo = t0 + T_OWN - (t_win + 1)
    seq = xfull[s]
    if lo >= 0:
        return _mk(seq[lo:t0 + T_OWN].astype(np.float32))
    pad = np.zeros((-lo, C), np.float32)
    return _mk(np.concatenate([pad, seq[0:t0 + T_OWN]]).astype(np.float32))


def kernel(**inputs):
    shared = _prep_shared(inputs)
    xfull = np.asarray(inputs["L0_attn_x"], dtype=np.float32)
    n_pre, n_own = _N_PRE, _N_OWN
    t_win = (n_pre + n_own) * MAC
    nc = build_program(n_mac_pre=n_pre, n_mac_own=n_own, do_ffn=_DO_FFN, debug=_DEBUG)
    in_maps = []
    for core in range(N_CORE):
        mm = dict(shared)
        mm["xin"] = _core_x(xfull, core, t_win)
        in_maps.append(mm)
    res = run_bass_kernel_spmd(nc, in_maps, core_ids=list(range(N_CORE)))
    _LAST_RES[0] = res
    out = np.zeros((2, 8192, C), np.float32)
    for core in range(N_CORE):
        s, p = core // 4, core % 4
        out[s, p * T_OWN:(p + 1) * T_OWN] = np.asarray(res.results[core]["out"], dtype=np.float32)
    return out
